# version 02 (final)
# baseline (speedup 1.0000x reference)
import numpy as np
import ml_dtypes
from contextlib import ExitStack
import concourse.bass as bass
import concourse.mybir as mybir
from concourse.bass_utils import run_bass_kernel_spmd

F32 = mybir.dt.float32
BF16 = mybir.dt.bfloat16
I16 = mybir.dt.int16
ALU = mybir.AluOpType
ACT = mybir.ActivationFunctionType
AX = mybir.AxisListType

NC = 8
T = 2048
TALL = 16384
D = 1024
E = 4096
DI = 2048
NS = 16
RK = 64
FF = 2816
DS = DI // NC
HALO = 3
TH = T + HALO
MAGIC = float(3 << 22)
QMAX = 127.0
EPS = 1e-5
KD = D // 128
MI = DI // 128
ME = E // 128
MF = FF // 128

NBF = np.dtype(ml_dtypes.bfloat16)
_FFB_DEBUG = False


def nbl(total, blk=512):
    out, s = [], 0
    while s < total:
        out.append((s, min(blk, total - s)))
        s += blk
    return out



def build_modvec():
    nc = bass.Bass('TRN2', target_bir_lowering=False, detect_race_conditions=False)
    JS = 6144 // NC
    JC = JS // 128
    w0 = nc.dram_tensor('w0', [D, JS], F32, kind='ExternalInput').ap()
    w1 = nc.dram_tensor('w1', [D, JS], F32, kind='ExternalInput').ap()
    c0 = nc.dram_tensor('c0', [128, 8], F32, kind='ExternalInput').ap()
    c1 = nc.dram_tensor('c1', [128, 8], F32, kind='ExternalInput').ap()
    mvo = nc.dram_tensor('mvo', [128, 2 * JC], F32, kind='ExternalOutput').ap()

    with (
        nc.sbuf_tensor('wt', [128, KD * JS], F32) as wt,
        nc.sbuf_tensor('cs', [128, 16], F32) as cs,
        nc.sbuf_tensor('mv', [128, 2 * JC], F32) as mv,
        nc.psum_tensor('pm', [128, 2 * JC], F32) as pm,
        nc.semaphore('ld') as ld,
        nc.semaphore('mm') as mm,
        nc.semaphore('sp') as sp,
        nc.semaphore('vp') as vp,
        nc.semaphore('so') as so,
        nc.Block() as block,
    ):
        @block.sync
        def _(sync):
            sync.dma_start(out=cs[:, 0:8], in_=c0).then_inc(ld, 16)
            sync.dma_start(out=cs[:, 8:16], in_=c1).then_inc(ld, 16)
            sync.dma_start(out=wt[:].rearrange("p (k j) -> p k j", k=KD), in_=w0.rearrange("(k p) j -> p k j", p=128)).then_inc(ld, 16)

        @block.scalar
        def _(scalar):
            scalar.wait_ge(ld, 48)
            scalar.activation(out=cs[:], in_=cs[:], func=ACT.Silu).then_inc(sp, 1)

        @block.gpsimd
        def _(gpsimd):
            gpsimd.wait_ge(mm, JC)
            gpsimd.dma_start(out=wt[:].rearrange("p (k j) -> p k j", k=KD), in_=w1.rearrange("(k p) j -> p k j", p=128)).then_inc(ld, 16)
            gpsimd.wait_ge(vp, 1)
            gpsimd.dma_start(out=mvo, in_=mv[:]).then_inc(so, 16)
            gpsimd.wait_ge(so, 16)

        @block.tensor
        def _(tensor):
            tensor.wait_ge(ld, 48)
            tensor.wait_ge(sp, 1)
            for l in range(2):
                if l == 1:
                    tensor.wait_ge(ld, 64)
                for jc in range(JC):
                    for kc in range(KD):
                        ins = tensor.matmul(pm[:, l * JC + jc:l * JC + jc + 1],
                                            wt[:, kc * JS + jc * 128:kc * JS + jc * 128 + 128],
                                            cs[:, l * 8 + kc:l * 8 + kc + 1],
                                            start=(kc == 0), stop=(kc == KD - 1))
                    ins.then_inc(mm, 1)

        @block.vector
        def _(vector):
            vector.wait_ge(mm, 2 * JC)
            vector.tensor_copy(out=mv[:], in_=pm[:]).then_inc(vp, 1)

    return nc



def build_pre(layer):
    nc = bass.Bass('TRN2', target_bir_lowering=False, detect_race_conditions=False)
    two_ln = (layer == 1)

    xin = nc.dram_tensor('xin', [D, TH], F32, kind='ExternalInput').ap()
    hs = nc.dram_tensor('hs', [128, 1], F32, kind='ExternalInput').ap()
    mvin = nc.dram_tensor('mvin', [128, 48], F32, kind='ExternalInput').ap()
    winT = nc.dram_tensor('winT', [D, E], F32, kind='ExternalInput').ap()
    cw = nc.dram_tensor('cw', [128, 64], F32, kind='ExternalInput').ap()
    cb = nc.dram_tensor('cb', [128, 16], F32, kind='ExternalInput').ap()
    xwT = nc.dram_tensor('xwT', [DI, 96], F32, kind='ExternalInput').ap()
    if two_ln:
        gam = nc.dram_tensor('gam', [128, 8], F32, kind='ExternalInput').ap()
        bet = nc.dram_tensor('bet', [128, 8], F32, kind='ExternalInput').ap()

    xio = nc.dram_tensor('xio', [DI, T], BF16, kind='ExternalOutput').ap()
    zo = nc.dram_tensor('zo', [DI, T], BF16, kind='ExternalOutput').ap()
    dtbco = nc.dram_tensor('dtbco', [96, T], F32, kind='ExternalOutput').ap()
    mvdo = nc.dram_tensor('mvdo', [128, 48], F32, kind='ExternalOutput').ap()
    lnso = nc.dram_tensor('lnso', [2, T], F32, kind='ExternalOutput').ap()

    nbs_h = nbl(TH)
    NBH = len(nbs_h)
    nbs_q = nbl(T)

    ring = []
    n_pass = 2 if two_ln else 1
    for p_ in range(n_pass):
        for k in range(KD):
            for bi, (o, n) in enumerate(nbs_h):
                ring.append(('su', p_, k, bi, o, n))
                ring.append(('sq', p_, k, bi, o, n))
    stats_steps = len(ring)
    for m in range(ME):
        for bi, (o, n) in enumerate(nbs_h):
            ring.append(('xz' if m < MI else 'z', m, bi, o, n))
    NRING = len(ring)
    ordv, ords = {}, {}
    cv = cs_ = 0
    for i2, st in enumerate(ring):
        if st[0] == 'z':
            cs_ += 1
        else:
            cv += 1
        ordv[i2], ords[i2] = cv, cs_
    NPS = 4

    vr_app1 = 3
    vr_rows2 = 4 + KD
    vr_app2 = 5 + KD
    vr_wc0 = (5 + 2 * KD) if two_ln else (3 + KD)
    vr_xw0 = vr_wc0 + KD
    vr_ring0 = vr_xw0 + MI
    n_small = 6 if two_ln else 4

    def ld_stat(k):
        return (n_small + k + 1) * 16

    def ld_app(k):
        return (n_small + KD + k + 1) * 16

    def ld_w(k):
        return (k + 1) * 16

    def ld_xw(c):
        return (c + 1) * 16

    with ExitStack() as ctx:
        stg = ctx.enter_context(nc.sbuf_tensor('stg', [128, TH], F32))
        scr = ctx.enter_context(nc.sbuf_tensor('scr', [128, 4 * TH], BF16))
        hbuf = ctx.enter_context(nc.sbuf_tensor('hbuf', [128, KD * TH], BF16))
        wst = ctx.enter_context(nc.sbuf_tensor('wst', [128, E], F32))
        wbf = ctx.enter_context(nc.sbuf_tensor('wbf', [128, KD * E], BF16))
        xwst = ctx.enter_context(nc.sbuf_tensor('xwst', [128, 96], F32))
        xwb = ctx.enter_context(nc.sbuf_tensor('xwb', [128, MI * 96], BF16))
        xz = ctx.enter_context(nc.sbuf_tensor('xz', [128, TH], F32))
        cacc = ctx.enter_context(nc.sbuf_tensor('cacc', [128, TH], F32))
        asc = ctx.enter_context(nc.sbuf_tensor('asc', [128, TH], F32))
        rows = ctx.enter_context(nc.sbuf_tensor('rows', [1, 4 * TH], F32))
        mv = ctx.enter_context(nc.sbuf_tensor('mv', [128, 48], F32))
        mvd = ctx.enter_context(nc.sbuf_tensor('mvd', [128, 48], F32))
        msk = ctx.enter_context(nc.sbuf_tensor('msk', [128, 8], F32))
        cwt = ctx.enter_context(nc.sbuf_tensor('cwt', [128, 64], F32))
        cbt = ctx.enter_context(nc.sbuf_tensor('cbt', [128, 16], F32))
        gbt = ctx.enter_context(nc.sbuf_tensor('gbt', [128, 16], F32))
        hsc = ctx.enter_context(nc.sbuf_tensor('hsc', [128, 1], F32))
        onesb = ctx.enter_context(nc.sbuf_tensor('onesb', [128, 1], BF16))
        onesr = ctx.enter_context(nc.sbuf_tensor('onesr', [1, 128], F32))
        p0 = ctx.enter_context(nc.psum_tensor('p0', [128, 512], F32))
        p1 = ctx.enter_context(nc.psum_tensor('p1', [128, 512], F32))
        p2 = ctx.enter_context(nc.psum_tensor('p2', [128, 512], F32))
        p3 = ctx.enter_context(nc.psum_tensor('p3', [128, 512], F32))
        q0 = ctx.enter_context(nc.psum_tensor('q0', [128, 512], F32))
        q1 = ctx.enter_context(nc.psum_tensor('q1', [128, 512], F32))
        q2 = ctx.enter_context(nc.psum_tensor('q2', [128, 512], F32))
        q3 = ctx.enter_context(nc.psum_tensor('q3', [128, 512], F32))
        ld = ctx.enter_context(nc.semaphore('ld'))
        sqm = ctx.enter_context(nc.semaphore('sq'))
        sr = ctx.enter_context(nc.semaphore('sr'))
        vr = ctx.enter_context(nc.semaphore('vr'))
        br = ctx.enter_context(nc.semaphore('br'))
        rmm = ctx.enter_context(nc.semaphore('rmm'))
        xda = ctx.enter_context(nc.semaphore('xda'))
        evv = ctx.enter_context(nc.semaphore('evv'))
        evs = ctx.enter_context(nc.semaphore('evs'))
        vc = ctx.enter_context(nc.semaphore('vc'))
        sv = ctx.enter_context(nc.semaphore('sv'))
        sxia = ctx.enter_context(nc.semaphore('sxia'))
        sxib = ctx.enter_context(nc.semaphore('sxib'))
        sz = ctx.enter_context(nc.semaphore('sz'))
        sot = ctx.enter_context(nc.semaphore('sot'))
        vfl = ctx.enter_context(nc.semaphore('vfl'))
        mbc = ctx.enter_context(nc.semaphore('mbc'))
        sbv = ctx.enter_context(nc.semaphore('sbv'))
        ldwp = ctx.enter_context(nc.semaphore('ldwp'))
        ldxw = ctx.enter_context(nc.semaphore('ldxw'))
        block = ctx.enter_context(nc.Block())

        psum = [p0, p1, p2, p3]
        qsum = [q0, q1, q2, q3]
        sxi = [sxia, sxib]

        aB = xz
        bB = cacc

        def XIRES(m2):
            return scr[:, (m2 % 2) * T:(m2 % 2) * T + T]

        def ZT(o, n):
            return scr[:, 2 * T + o:2 * T + o + n]

        def XDBT(qo, qn):
            return stg[0:96, qo:qo + qn]

        def R(r):
            return rows[0:1, r * TH:(r + 1) * TH]

        def Rs(r, o, n):
            return rows[0:1, r * TH + o:r * TH + o + n]

        @block.sync
        def _(sync):
            sync.dma_start(out=hsc[:], in_=hs).then_inc(ld, 16)
            sync.dma_start(out=mv[:], in_=mvin).then_inc(ld, 16)
            sync.dma_start(out=cwt[:], in_=cw).then_inc(ld, 16)
            sync.dma_start(out=cbt[:], in_=cb).then_inc(ld, 16)
            if two_ln:
                sync.dma_start(out=gbt[:, 0:8], in_=gam).then_inc(ld, 16)
                sync.dma_start(out=gbt[:, 8:16], in_=bet).then_inc(ld, 16)
            for k in range(KD):
                if k >= 1:
                    sync.wait_ge(sqm, k)
                sync.dma_start(out=stg[:, :],
                               in_=xin[k * 128:(k + 1) * 128, :]).then_inc(ld, 16)
            for k in range(KD):
                sync.wait_ge(sqm, KD)
                if k >= 1:
                    sync.wait_ge(vr, vr_app1 + k)
                sync.dma_start(out=stg[:, :],
                               in_=xin[k * 128:(k + 1) * 128, :]).then_inc(ld, 16)
            for k in range(KD):
                if k >= 1:
                    sync.wait_ge(vr, vr_wc0 + k)
                sync.dma_start(out=wst[:, :],
                               in_=winT[k * 128:(k + 1) * 128, :]).then_inc(ldwp, 16)
            for c in range(MI):
                if c == 0:
                    sync.wait_ge(vr, vr_wc0 + KD)
                else:
                    sync.wait_ge(vr, vr_xw0 + c)
                sync.dma_start(out=xwst[:, :],
                               in_=xwT[c * 128:(c + 1) * 128, :]).then_inc(ldxw, 16)

        @block.scalar
        def _(scalar):
            for k in range(KD):
                scalar.wait_ge(ld, ld_stat(k))
                if k >= 2:
                    scalar.wait_ge(rmm, (k - 1) * 2 * NBH)
                scalar.activation(out=scr[:, (k % 2) * TH:(k % 2) * TH + TH], in_=stg[:, :],
                                  func=ACT.Copy)
                scalar.activation(out=scr[:, (2 + k % 2) * TH:(2 + k % 2) * TH + TH],
                                  in_=stg[:, :], func=ACT.Square).then_inc(sqm, 1)
            def bcast_evacs(base_j):
                for bi, (o, n) in enumerate(nbs_h):
                    for ri in range(2):
                        j = base_j + bi * 2 + ri
                        scalar.wait_ge(mbc, j + 1)
                        dst = (aB if ri == 0 else bB)[:, o:o + n]
                        scalar.activation(out=dst, in_=qsum[j % 2][:, :n],
                                          func=ACT.Copy).then_inc(sbv, 1)

            scalar.wait_ge(vr, 2)
            scalar.activation(out=R(2), in_=R(2), func=ACT.Ln)
            scalar.activation(out=R(2), in_=R(2), func=ACT.Exp, scale=-0.5).then_inc(sr, 1)
            bcast_evacs(0)
            if two_ln:
                for k in range(KD):
                    scalar.wait_ge(vr, vr_app1 + k + 1)
                    scalar.wait_ge(rmm, stats_steps // 2 + max(0, (k - 1)) * 2 * NBH)
                    sl = hbuf[:, k * TH:(k + 1) * TH]
                    scalar.activation(out=scr[:, (k % 2) * TH:(k % 2) * TH + TH], in_=sl,
                                      func=ACT.Copy)
                    scalar.activation(out=scr[:, (2 + k % 2) * TH:(2 + k % 2) * TH + TH],
                                      in_=sl, func=ACT.Square).then_inc(sqm, 1)
                scalar.wait_ge(vr, vr_rows2)
                scalar.activation(out=R(2), in_=R(2), func=ACT.Ln)
                scalar.activation(out=R(2), in_=R(2), func=ACT.Exp, scale=-0.5).then_inc(sr, 1)
                bcast_evacs(2 * NBH)
            for m2 in range(MI):
                scalar.wait_ge(vc, m2 + 1)
                if m2 >= 2:
                    scalar.wait_ge(xda, m2 - 1)
                    scalar.wait_ge(sxi[m2 % 2], (m2 // 2) * 16)
                scalar.activation(out=XIRES(m2), in_=cacc[:, 0:T], func=ACT.Silu,
                                  bias=cbt[:, m2:m2 + 1]).then_inc(sv, 1)
            for i2 in range(stats_steps, NRING):
                kind, m, bi, o, n = ring[i2]
                if kind != 'z':
                    continue
                mz = m - MI
                scalar.wait_ge(rmm, i2 + 1)
                if bi == 0 and mz >= 1:
                    scalar.wait_ge(sz, mz * 16)
                scalar.activation(out=ZT(o, n), in_=psum[i2 % NPS][:, :n],
                                  func=ACT.Copy).then_inc(evs, 1)

        @block.tensor
        def _(tensor):
            def bcast_mms(base_j, vr_gate):
                tensor.wait_ge(vr, vr_gate)
                for bi, (o, n) in enumerate(nbs_h):
                    for ri in range(2):
                        j = base_j + bi * 2 + ri
                        if j >= 2:
                            tensor.wait_ge(sbv, j - 1)
                        tensor.matmul(qsum[j % 2][:, :n], onesr[0:1, :],
                                      Rs(2 + ri, o, n), start=True,
                                      stop=True).then_inc(mbc, 1)

            tensor.wait_ge(vr, 1)
            for i2 in range(NRING):
                kind = ring[i2][0]
                if two_ln and i2 == stats_steps // 2:
                    bcast_mms(0, 3)
                if i2 == stats_steps:
                    if two_ln:
                        bcast_mms(2 * NBH, vr_app2)
                    else:
                        bcast_mms(0, 3)
                    tensor.wait_ge(vr, vr_ring0)
                if i2 >= NPS:
                    pk = ring[i2 - NPS][0]
                    if pk == 'z':
                        tensor.wait_ge(evs, ords[i2 - NPS])
                    else:
                        tensor.wait_ge(evv, ordv[i2 - NPS])
                out = psum[i2 % NPS]
                if kind in ('su', 'sq'):
                    _, p_, k, bi, o, n = ring[i2]
                    tensor.wait_ge(sqm, p_ * KD + k + 1)
                    base = (0 if kind == 'su' else 2) + k % 2
                    src = scr[:, base * TH + o:base * TH + o + n]
                    ins = tensor.matmul(out[:1, :n], onesb[:, 0:1], src, start=True, stop=True)
                    ins.then_inc(rmm, 1)
                else:
                    _, m, bi, o, n = ring[i2]
                    for k in range(KD):
                        ins = tensor.matmul(out[:, :n], wbf[:, k * E + m * 128:k * E + (m + 1) * 128],
                                            hbuf[:, k * TH + o:k * TH + o + n],
                                            start=(k == 0), stop=(k == KD - 1))
                    ins.then_inc(rmm, 1)
                    if kind == 'xz' and bi == NBH - 1:
                        m2 = m
                        tensor.wait_ge(sv, m2 + 1)
                        for qb, (qo, qn) in enumerate(nbs_q):
                            qi = tensor.matmul(qsum[qb][:96, :qn],
                                               xwb[:, m2 * 96:(m2 + 1) * 96],
                                               XIRES(m2)[:, qo:qo + qn],
                                               start=(m2 == 0), stop=(m2 == MI - 1))
                        qi.then_inc(xda, 1)

        @block.vector
        def _(vector):
            vector.wait_ge(ld, n_small * 16)
            vfl_c = [0]

            def vsync(ins):
                vfl_c[0] += 1
                ins.then_inc(vfl, 1)
                vector.wait_ge(vfl, vfl_c[0])

            vector.tensor_copy(out=mvd[:], in_=mv[:])
            vector.tensor_scalar(out=mvd[:, 8:16], in0=mv[:, 8:16], scalar1=1.0,
                                 scalar2=None, op0=ALU.add)
            vector.tensor_scalar(out=mvd[:, 32:40], in0=mv[:, 32:40], scalar1=1.0,
                                 scalar2=None, op0=ALU.add)
            for base in (16, 40):
                vsync(vector.tensor_scalar(out=msk[:], in0=mv[:, base:base + 8], scalar1=0.0,
                                           scalar2=None, op0=ALU.is_ge))
                vsync(vector.tensor_scalar(out=msk[:], in0=msk[:], scalar1=2.0, scalar2=-1.0,
                                           op0=ALU.mult, op1=ALU.add))
                vsync(vector.tensor_tensor(out=mvd[:, base:base + 8], in0=mv[:, base:base + 8],
                                           in1=msk[:], op=ALU.mult))
                vsync(vector.tensor_scalar(out=mvd[:, base:base + 8], in0=mvd[:, base:base + 8],
                                           scalar1=0.25, scalar2=None, op0=ALU.max))
                vsync(vector.tensor_tensor(out=mvd[:, base:base + 8], in0=mvd[:, base:base + 8],
                                           in1=msk[:], op=ALU.mult))
            vector.memset(onesb[:], 1.0)
            vector.memset(onesr[:], 1.0)
            vector.engine_nop().then_inc(vr, 1)

            def stats_consume(pass_):
                base = pass_ * 2 * KD * NBH
                for k in range(KD):
                    for bi, (o, n) in enumerate(nbs_h):
                        for wi, r in ((0, 0), (1, 1)):
                            i2 = base + k * 2 * NBH + bi * 2 + wi
                            vector.wait_ge(rmm, i2 + 1)
                            ps = psum[i2 % NPS]
                            if k == 0:
                                vector.tensor_copy(out=Rs(r, o, n),
                                                   in_=ps[0:1, :n]).then_inc(evv, 1)
                            else:
                                vector.tensor_tensor(out=Rs(r, o, n), in0=Rs(r, o, n),
                                                     in1=ps[0:1, :n],
                                                     op=ALU.add).then_inc(evv, 1)

            def rows_math():
                vector.tensor_scalar(out=R(0), in0=R(0), scalar1=1.0 / D,
                                     scalar2=None, op0=ALU.mult)
                vector.tensor_scalar(out=R(1), in0=R(1), scalar1=1.0 / D,
                                     scalar2=None, op0=ALU.mult)
                vector.tensor_tensor(out=R(2), in0=R(0), in1=R(0), op=ALU.mult)
                vector.tensor_tensor(out=R(2), in0=R(1), in1=R(2), op=ALU.subtract)
                vector.tensor_scalar(out=R(2), in0=R(2), scalar1=EPS,
                                     scalar2=None, op0=ALU.add).then_inc(vr, 1)

            def mrstd(srv):
                vector.wait_ge(sr, srv)
                vector.tensor_tensor(out=R(3), in0=R(0), in1=R(2), op=ALU.mult).then_inc(vr, 1)

            stats_consume(0)
            rows_math()
            mrstd(1)
            for k in range(KD):
                vector.wait_ge(ld, ld_app(k))
                if k == 0:
                    vector.wait_ge(sbv, 2 * NBH)
                vector.tensor_tensor(out=asc[:, :], in0=stg[:, :], in1=aB[:, :], op=ALU.mult)
                vector.tensor_tensor(out=asc[:, :], in0=asc[:, :], in1=bB[:, :], op=ALU.subtract)
                vector.tensor_scalar(out=hbuf[:, k * TH:(k + 1) * TH], in0=asc[:, :],
                                     scalar1=mvd[:, 8 + k:9 + k], scalar2=mvd[:, 0 + k:1 + k],
                                     op0=ALU.mult, op1=ALU.add).then_inc(vr, 1)
            if two_ln:
                vector.wait_ge(sot, 48)
                stats_consume(1)
                rows_math()
                mrstd(2)
                for k in range(KD):
                    if k == 0:
                        vector.wait_ge(sbv, 4 * NBH)
                    sl = slice(k * TH, (k + 1) * TH)
                    vector.tensor_tensor(out=asc[:, :], in0=hbuf[:, sl], in1=aB[:, :], op=ALU.mult)
                    vector.tensor_tensor(out=asc[:, :], in0=asc[:, :], in1=bB[:, :], op=ALU.subtract)
                    vector.tensor_scalar(out=hbuf[:, sl], in0=asc[:, :],
                                         scalar1=gbt[:, 0 + k:1 + k], scalar2=gbt[:, 8 + k:9 + k],
                                         op0=ALU.mult, op1=ALU.add).then_inc(vr, 1)
            for k in range(KD):
                vector.wait_ge(ldwp, ld_w(k))
                vector.tensor_copy(out=wbf[:, k * E:(k + 1) * E], in_=wst[:, :]).then_inc(vr, 1)
            for c in range(MI):
                vector.wait_ge(ldxw, ld_xw(c))
                vector.tensor_copy(out=xwb[:, c * 96:(c + 1) * 96],
                                   in_=xwst[:, :]).then_inc(vr, 1)
            for i2 in range(stats_steps, NRING):
                kind, m, bi, o, n = ring[i2]
                if kind != 'xz':
                    continue
                vector.wait_ge(rmm, i2 + 1)
                vector.tensor_copy(out=xz[:, o:o + n], in_=psum[i2 % NPS][:, :n]).then_inc(evv, 1)
                if bi == NBH - 1:
                    m2 = m
                    if m2 >= 1:
                        vector.wait_ge(sv, m2)
                    vector.tensor_scalar(out=xz[:, 0:HALO], in0=xz[:, 0:HALO],
                                         scalar1=hsc[:, 0:1], scalar2=None, op0=ALU.mult)
                    vector.tensor_scalar(out=cacc[:, 0:T], in0=xz[:, 0:T],
                                         scalar1=cwt[:, m2 * 4:m2 * 4 + 1], scalar2=None,
                                         op0=ALU.mult)
                    for kk in range(1, 4):
                        vector.scalar_tensor_tensor(out=cacc[:, 0:T], in0=xz[:, kk:kk + T],
                                                    scalar=cwt[:, m2 * 4 + kk:m2 * 4 + kk + 1],
                                                    in1=cacc[:, 0:T], op0=ALU.mult, op1=ALU.add)
                    vector.engine_nop().then_inc(vc, 1)
            vector.wait_ge(xda, MI)
            for qb, (qo, qn) in enumerate(nbs_q):
                vector.tensor_copy(out=XDBT(qo, qn), in_=qsum[qb][:96, :qn])
            vector.engine_nop().then_inc(vr, 1)

        @block.gpsimd
        def _(gpsimd):
            gpsimd.wait_ge(vr, 1)
            gpsimd.dma_start(out=mvdo, in_=mvd[:]).then_inc(sot, 16)
            gpsimd.wait_ge(vr, 3)
            gpsimd.dma_start(out=lnso[0:1, :], in_=Rs(0, HALO, T)).then_inc(sot, 16)
            gpsimd.dma_start(out=lnso[1:2, :], in_=Rs(2, HALO, T)).then_inc(sot, 16)
            for m2 in range(MI):
                gpsimd.wait_ge(xda, m2 + 1)
                gpsimd.dma_start(out=xio[m2 * 128:(m2 + 1) * 128, :],
                                 in_=XIRES(m2)).then_inc(sxi[m2 % 2], 16)
            for mz in range(MI):
                gpsimd.wait_ge(evs, (mz + 1) * NBH)
                gpsimd.dma_start(out=zo[mz * 128:(mz + 1) * 128, :],
                                 in_=ZT(HALO, T)).then_inc(sz, 16)
            gpsimd.wait_ge(vr, vr_ring0 + 1)
            gpsimd.dma_start(out=dtbco, in_=XDBT(0, T)).then_inc(sot, 16)
            gpsimd.wait_ge(sxia, MI * 8)
            gpsimd.wait_ge(sxib, MI * 8)
            gpsimd.wait_ge(sz, MI * 16)
            gpsimd.wait_ge(sot, 64)

    return nc



TB = 2048
NTB = TALL // TB


def build_scan(layer):
    nc = bass.Bass('TRN2', target_bir_lowering=False, detect_race_conditions=False)
    two = (layer == 1)

    xi_sl = nc.dram_tensor('xi_sl', [DS, TALL], BF16, kind='ExternalInput').ap()
    dt_all = nc.dram_tensor('dt_all', [RK, TALL], F32, kind='ExternalInput').ap()
    bc_all = nc.dram_tensor('bc_all', [2 * NS, TALL], F32, kind='ExternalInput').ap()
    dtw = nc.dram_tensor('dtw', [RK, DS], F32, kind='ExternalInput').ap()
    acol_in = nc.dram_tensor('acol_in', [128, 2 * NS], F32, kind='ExternalInput').ap()
    selm = nc.dram_tensor('selm', [2 * NS, 2 * NS * 128], F32, kind='ExternalInput').ap()
    if two:
        dtb = nc.dram_tensor('dtb', [128, 2], F32, kind='ExternalInput').ap()
    y_sl = nc.dram_tensor('y_sl', [DS, TALL], BF16, kind='ExternalOutput').ap()

    NGRP = 4

    with ExitStack() as ctx:
        xib = ctx.enter_context(nc.sbuf_tensor('xib', [128, 2 * 2 * TB], BF16))
        dtb_s = ctx.enter_context(nc.sbuf_tensor('dtb_s', [RK, 2 * TB], F32))
        dlt = ctx.enter_context(nc.sbuf_tensor('dlt', [128, 2 * TB], F32))
        du = ctx.enter_context(nc.sbuf_tensor('du', [128, 2 * TB], BF16))
        dab = ctx.enter_context(nc.sbuf_tensor('dab', [128, 4 * TB], F32))
        dbu = ctx.enter_context(nc.sbuf_tensor('dbu', [128, 2 * TB], BF16))
        ht = ctx.enter_context(nc.sbuf_tensor('ht', [128, 8 * TB], BF16))
        pg = ctx.enter_context(nc.sbuf_tensor('pg', [128, 8 * TB], BF16))
        yt = ctx.enter_context(nc.sbuf_tensor('yt', [128, 2 * TB], BF16))
        bcs = ctx.enter_context(nc.sbuf_tensor('bcs', [2 * NS, 2 * TB], F32))
        selt = ctx.enter_context(nc.sbuf_tensor('selt', [2 * NS, 2 * NS * 128], F32))
        st32 = ctx.enter_context(nc.sbuf_tensor('st32', [128, 2 * NS], F32))
        acol = ctx.enter_context(nc.sbuf_tensor('acol', [128, 2 * NS], F32))
        dtwt = ctx.enter_context(nc.sbuf_tensor('dtwt', [RK, DS], F32))
        dtbc = ctx.enter_context(nc.sbuf_tensor('dtbc', [128, 2], F32))
        tcol = ctx.enter_context(nc.sbuf_tensor('tcol', [128, 1], F32))
        p0 = ctx.enter_context(nc.psum_tensor('p0', [128, 512], F32))
        p1 = ctx.enter_context(nc.psum_tensor('p1', [128, 512], F32))
        p2 = ctx.enter_context(nc.psum_tensor('p2', [128, 512], F32))
        p3 = ctx.enter_context(nc.psum_tensor('p3', [128, 512], F32))
        pB0 = ctx.enter_context(nc.psum_tensor('pB0', [128, 512], F32))
        pB1 = ctx.enter_context(nc.psum_tensor('pB1', [128, 512], F32))
        pC0 = ctx.enter_context(nc.psum_tensor('pC0', [128, 512], F32))
        pC1 = ctx.enter_context(nc.psum_tensor('pC1', [128, 512], F32))
        lds = ctx.enter_context(nc.semaphore('lds'))
        ldxi0 = ctx.enter_context(nc.semaphore('ldxi0'))
        ldxi1 = ctx.enter_context(nc.semaphore('ldxi1'))
        lddt0 = ctx.enter_context(nc.semaphore('lddt0'))
        lddt1 = ctx.enter_context(nc.semaphore('lddt1'))
        ldbc0 = ctx.enter_context(nc.semaphore('ldbc0'))
        ldbc1 = ctx.enter_context(nc.semaphore('ldbc1'))
        pmm = ctx.enter_context(nc.semaphore('pmm'))
        sfp = ctx.enter_context(nc.semaphore('sfp'))
        da = ctx.enter_context(nc.semaphore('da'))
        aneg = ctx.enter_context(nc.semaphore('aneg'))
        vdu = ctx.enter_context(nc.semaphore('vdu'))
        vsc = ctx.enter_context(nc.semaphore('vsc'))
        vy = ctx.enter_context(nc.semaphore('vy'))
        sy = ctx.enter_context(nc.semaphore('sy'))
        mmB = ctx.enter_context(nc.semaphore('mmB'))
        mmC = ctx.enter_context(nc.semaphore('mmC'))
        vBf = ctx.enter_context(nc.semaphore('vBf'))
        vCf = ctx.enter_context(nc.semaphore('vCf'))
        vfl = ctx.enter_context(nc.semaphore('vfl'))
        block = ctx.enter_context(nc.Block())

        psum = [p0, p1, p2, p3]
        pB = [pB0, pB1]
        pC = [pC0, pC1]
        ldxi = [ldxi0, ldxi1]
        lddt = [lddt0, lddt1]
        ldbc = [ldbc0, ldbc1]
        NSM = (4 if two else 3) * 16

        def XI(tb, dhi):
            return xib[:, ((tb % 2) * 2 + dhi) * TB:((tb % 2) * 2 + dhi) * TB + TB]

        def DTB(tb):
            return dtb_s[0:RK, (tb % 2) * TB:(tb % 2) * TB + TB]

        def DLT(dhi):
            return dlt[:, dhi * TB:(dhi + 1) * TB]

        def DU(dhi):
            return du[:, dhi * TB:(dhi + 1) * TB]

        def DAB(idx):
            return dab[:, (idx % 4) * TB:((idx % 4) + 1) * TB]

        def DBU(idx):
            return dbu[:, (idx % 2) * TB:((idx % 2) + 1) * TB]

        def BCS(tb):
            return bcs[0:2 * NS, (tb % 2) * TB:(tb % 2) * TB + TB]

        def HT(dhi, ng):
            return ht[:, (dhi * 4 + ng) * TB:(dhi * 4 + ng + 1) * TB]

        def PG(dhi, g):
            return pg[:, (dhi * 4 + g) * TB:(dhi * 4 + g + 1) * TB]

        @block.sync
        def _(sync):
            sync.dma_start(out=dtwt[:], in_=dtw).then_inc(lds, 16)
            sync.dma_start(out=acol[:], in_=acol_in).then_inc(lds, 16)
            if two:
                sync.dma_start(out=dtbc[:], in_=dtb).then_inc(lds, 16)
            sync.dma_start(out=selt[:], in_=selm).then_inc(lds, 16)
            for c8 in (0, 1):
                sync.dma_start(out=BCS(c8),
                               in_=bc_all[:, c8 * TB:(c8 + 1) * TB]).then_inc(ldbc[c8 % 2], 16)
            for tb in range(NTB):
                if tb >= 2:
                    sync.wait_ge(vdu, tb - 1)
                sync.dma_start(out=XI(tb, 0),
                               in_=xi_sl[0:128, tb * TB:(tb + 1) * TB]).then_inc(ldxi[tb % 2], 16)
                sync.dma_start(out=XI(tb, 1),
                               in_=xi_sl[128:256, tb * TB:(tb + 1) * TB]).then_inc(ldxi[tb % 2], 16)
                if tb >= 2:
                    sync.wait_ge(pmm, 8 * (tb - 1))
                sync.dma_start(out=DTB(tb),
                               in_=dt_all[:, tb * TB:(tb + 1) * TB]).then_inc(lddt[tb % 2], 16)
                if tb + 2 < NTB:
                    c8 = tb + 2
                    sync.wait_ge(mmC, (c8 - 1) * 64)
                    sync.dma_start(out=BCS(c8),
                                   in_=bc_all[:, c8 * TB:(c8 + 1) * TB]).then_inc(ldbc[c8 % 2], 16)

        @block.tensor
        def _(tensor):
            def dtproj(tb):
                tensor.wait_ge(lddt[tb % 2], (tb // 2 + 1) * 16)
                for dhi in range(2):
                    for nb, (o, n) in enumerate(nbl(TB)):
                        dmi = tb * 8 + dhi * 4 + nb
                        if dmi >= 4:
                            tensor.wait_ge(sfp, dmi - 3)
                        ins = tensor.matmul(psum[dmi % 4][:, :n],
                                            dtwt[0:RK, dhi * 128:(dhi + 1) * 128],
                                            DTB(tb)[:, o:o + n],
                                            start=True, stop=True)
                        ins.then_inc(pmm, 1)

            tensor.wait_ge(lds, NSM)
            dtproj(0)
            for tb in range(NTB):
                tensor.wait_ge(ldbc[tb % 2], (tb // 2 + 1) * 16)
                for n_abs in range(2 * NS // 2):
                    bidx = tb * NS + n_abs
                    for b in range(4):
                        gB = bidx * 4 + b
                        if gB >= 2:
                            tensor.wait_ge(vBf, gB - 1)
                        tensor.matmul(pB[b % 2][:, :],
                                      selt[:, n_abs * 128:(n_abs + 1) * 128],
                                      BCS(tb)[:, b * 512:(b + 1) * 512],
                                      start=True, stop=True).then_inc(mmB, 1)
                    for b in range(4):
                        gC = bidx * 4 + b
                        if gC >= 2:
                            tensor.wait_ge(vCf, gC - 1)
                        tensor.matmul(pC[b % 2][:, :],
                                      selt[:, (NS + n_abs) * 128:(NS + n_abs + 1) * 128],
                                      BCS(tb)[:, b * 512:(b + 1) * 512],
                                      start=True, stop=True).then_inc(mmC, 1)
                if tb + 1 < NTB:
                    dtproj(tb + 1)

        @block.scalar
        def _(scalar):
            scalar.wait_ge(lds, NSM)
            scalar.activation(out=acol[:], in_=acol[:], func=ACT.Exp)
            scalar.wait_ge(aneg, 2 if two else 1)
            for tb in range(NTB):
                if tb >= 1:
                    scalar.wait_ge(vdu, tb)
                for dhi in range(2):
                    for nb, (o, n) in enumerate(nbl(TB)):
                        dmi = tb * 8 + dhi * 4 + nb
                        scalar.wait_ge(pmm, dmi + 1)
                        kw = dict(bias=dtbc[:, dhi:dhi + 1]) if two else {}
                        scalar.activation(out=DLT(dhi)[:, o:o + n], in_=psum[dmi % 4][:, :n],
                                          func=ACT.Sigmoid, scale=-1.0, **kw)
                        scalar.activation(out=DLT(dhi)[:, o:o + n], in_=DLT(dhi)[:, o:o + n],
                                          func=ACT.Ln, bias=tcol[:, 0:1]).then_inc(sfp, 1)
                for g in range(NGRP):
                    for ng in range(4):
                        for dhi in range(2):
                            idx = tb * 32 + (g * 4 + ng) * 2 + dhi
                            n_abs = g * 4 + ng
                            if idx >= 4:
                                scalar.wait_ge(vsc, idx - 3)
                            scalar.activation(out=DAB(idx), in_=DLT(dhi),
                                              func=ACT.Exp,
                                              scale=acol[:, dhi * NS + n_abs:dhi * NS + n_abs + 1]
                                              ).then_inc(da, 1)

        @block.vector
        def _(vector):
            vfl_c = [0]

            def vsync(ins):
                vfl_c[0] += 1
                ins.then_inc(vfl, 1)
                vector.wait_ge(vfl, vfl_c[0])

            vector.memset(st32[:], 0.0)
            vector.memset(tcol[:], 1e-38)
            vector.engine_nop().then_inc(aneg, 1)
            if two:
                vector.wait_ge(lds, NSM)
                vsync(vector.tensor_scalar(out=dtbc[:], in0=dtbc[:], scalar1=-1.0,
                                           scalar2=None, op0=ALU.mult))
                vector.engine_nop().then_inc(aneg, 1)
            for tb in range(NTB):
                for dhi in range(2):
                    dmi_last = tb * 8 + dhi * 4 + 4
                    vector.wait_ge(sfp, dmi_last)
                    vector.wait_ge(ldxi[tb % 2], (tb // 2 + 1) * 32)
                    vector.scalar_tensor_tensor(out=DU(dhi), in0=DLT(dhi), scalar=-1.0,
                                                in1=XI(tb, dhi), op0=ALU.mult, op1=ALU.mult)
                vector.engine_nop().then_inc(vdu, 1)
                if tb >= 1:
                    vector.wait_ge(sy, 32 * tb)
                for g in range(NGRP):
                    for ng in range(4):
                        n_abs = g * 4 + ng
                        bidx = tb * NS + n_abs
                        for b in range(4):
                            vector.wait_ge(mmB, bidx * 4 + b + 1)
                            for dhi in range(2):
                                idx = tb * 32 + n_abs * 2 + dhi
                                ins = vector.tensor_tensor(
                                    out=DBU(idx)[:, b * 512:(b + 1) * 512],
                                    in0=DU(dhi)[:, b * 512:(b + 1) * 512],
                                    in1=pB[b % 2][:, :], op=ALU.mult)
                            ins.then_inc(vBf, 1)
                        for dhi in range(2):
                            idx = tb * 32 + n_abs * 2 + dhi
                            vector.wait_ge(da, idx + 1)
                            sc = vector.tensor_tensor_scan(
                                out=HT(dhi, ng), data0=DAB(idx), data1=DBU(idx),
                                initial=st32[:, dhi * NS + n_abs:dhi * NS + n_abs + 1],
                                op0=ALU.mult, op1=ALU.add)
                            vsync(sc)
                            vector.tensor_copy(
                                out=st32[:, dhi * NS + n_abs:dhi * NS + n_abs + 1],
                                in_=HT(dhi, ng)[:, TB - 1:TB]).then_inc(vsc, 1)
                        for b in range(4):
                            vector.wait_ge(mmC, bidx * 4 + b + 1)
                            for dhi in range(2):
                                ins = vector.tensor_tensor(
                                    out=HT(dhi, ng)[:, b * 512:(b + 1) * 512],
                                    in0=HT(dhi, ng)[:, b * 512:(b + 1) * 512],
                                    in1=pC[b % 2][:, :], op=ALU.mult)
                            ins.then_inc(vCf, 1)
                    for dhi in range(2):
                        vector.tensor_tensor(out=HT(dhi, 0), in0=HT(dhi, 0), in1=HT(dhi, 1),
                                             op=ALU.add)
                        vector.tensor_tensor(out=HT(dhi, 2), in0=HT(dhi, 2), in1=HT(dhi, 3),
                                             op=ALU.add)
                        vector.tensor_tensor(out=PG(dhi, g), in0=HT(dhi, 0), in1=HT(dhi, 2),
                                             op=ALU.add)
                for dhi in range(2):
                    vector.tensor_tensor(out=PG(dhi, 0), in0=PG(dhi, 0), in1=PG(dhi, 1),
                                         op=ALU.add)
                    vector.tensor_tensor(out=PG(dhi, 2), in0=PG(dhi, 2), in1=PG(dhi, 3),
                                         op=ALU.add)
                    vector.tensor_tensor(out=yt[:, dhi * TB:(dhi + 1) * TB],
                                         in0=PG(dhi, 0), in1=PG(dhi, 2), op=ALU.add)
                vector.engine_nop().then_inc(vy, 1)

        @block.gpsimd
        def _(gpsimd):
            for tb in range(NTB):
                gpsimd.wait_ge(vy, tb + 1)
                gpsimd.dma_start(out=y_sl[0:128, tb * TB:(tb + 1) * TB],
                                 in_=yt[:, 0:TB]).then_inc(sy, 16)
                gpsimd.dma_start(out=y_sl[128:256, tb * TB:(tb + 1) * TB],
                                 in_=yt[:, TB:2 * TB]).then_inc(sy, 16)
            gpsimd.wait_ge(sy, NTB * 32)

    return nc


def run_scan(P, layer, xi_parts, dtbc_parts):
    ncb = build_scan(layer)
    xi_full = np.concatenate(xi_parts, axis=1)
    dtbc_full = np.concatenate(dtbc_parts, axis=1)
    dt_all = np.ascontiguousarray(dtbc_full[0:RK])
    bc_all = np.ascontiguousarray(dtbc_full[RK:96])
    alog = P['alog0'] if layer == 0 else P['alog1']
    dtwT = P['dtw0T'] if layer == 0 else P['dtw1']
    maps = []
    for j in range(NC):
        acol = np.ascontiguousarray(
            alog[j * DS:(j + 1) * DS].reshape(2, 128, NS).transpose(1, 0, 2).reshape(128, 2 * NS))
        m = dict(xi_sl=np.ascontiguousarray(xi_full[j * DS:(j + 1) * DS]),
                 dt_all=dt_all, bc_all=bc_all,
                 dtw=np.ascontiguousarray(dtwT[:, j * DS:(j + 1) * DS]),
                 acol_in=acol, selm=P['selm'])
        if layer == 1:
            m['dtb'] = np.ascontiguousarray(
                P['dtb1'][j * DS:(j + 1) * DS].reshape(2, 128).T)
        maps.append(m)
    res = run_bass_kernel_spmd(ncb, maps, core_ids=list(range(NC)))
    y_full = np.concatenate([res.results[j]['y_sl'] for j in range(NC)], axis=0)
    return y_full



def build_post0():
    nc = bass.Bass('TRN2', target_bir_lowering=False, detect_race_conditions=False)
    x0f = nc.dram_tensor('x0f', [D, T], F32, kind='ExternalInput').ap()
    y0 = nc.dram_tensor('y0', [DI, T], BF16, kind='ExternalInput').ap()
    xi0 = nc.dram_tensor('xi0', [DI, T], BF16, kind='ExternalInput').ap()
    z0 = nc.dram_tensor('z0', [DI, T], BF16, kind='ExternalInput').ap()
    dcol = nc.dram_tensor('dcol', [128, 16], F32, kind='ExternalInput').ap()
    woutT = nc.dram_tensor('woutT', [DI, D], F32, kind='ExternalInput').ap()
    mvd = nc.dram_tensor('mvd', [128, 48], F32, kind='ExternalInput').ap()
    x1o = nc.dram_tensor('x1o', [D, T], F32, kind='ExternalOutput').ap()

    NQ = 4
    QW = T // NQ

    def ld_gk(k):
        return (k // 2 + 1) * 48

    def ld_w(k):
        return (k + 1) * 16

    with ExitStack() as ctx:
        yb = ctx.enter_context(nc.sbuf_tensor('yb', [128, 2 * T], BF16))
        xib = ctx.enter_context(nc.sbuf_tensor('xib', [128, 2 * T], BF16))
        zb = ctx.enter_context(nc.sbuf_tensor('zb', [128, 2 * T], BF16))
        szt = ctx.enter_context(nc.sbuf_tensor('szt', [128, T], BF16))
        gtb = ctx.enter_context(nc.sbuf_tensor('gtb', [128, MI * T], BF16))
        wst = ctx.enter_context(nc.sbuf_tensor('wst', [128, D], F32))
        wob = ctx.enter_context(nc.sbuf_tensor('wob', [128, MI * D], BF16))
        x0t = ctx.enter_context(nc.sbuf_tensor('x0t', [128, 2 * QW], F32))
        x1t = ctx.enter_context(nc.sbuf_tensor('x1t', [128, 2 * QW], F32))
        dct = ctx.enter_context(nc.sbuf_tensor('dct', [128, 16], F32))
        mvt = ctx.enter_context(nc.sbuf_tensor('mvt', [128, 48], F32))
        ps = [ctx.enter_context(nc.psum_tensor(f'ps{i}', [128, QW], F32)) for i in range(8)]
        ld = ctx.enter_context(nc.semaphore('ld'))
        lt0 = ctx.enter_context(nc.semaphore('lt0'))
        lt1 = ctx.enter_context(nc.semaphore('lt1'))
        ldw = ctx.enter_context(nc.semaphore('ldw'))
        sz_ = ctx.enter_context(nc.semaphore('sz'))
        vg = ctx.enter_context(nc.semaphore('vg'))
        wc = ctx.enter_context(nc.semaphore('wc'))
        mmq = ctx.enter_context(nc.semaphore('mmq'))
        ve = ctx.enter_context(nc.semaphore('ve'))
        sx0 = ctx.enter_context(nc.semaphore('sx0'))
        sx1 = ctx.enter_context(nc.semaphore('sx1'))
        ldx = ctx.enter_context(nc.semaphore('ldx'))
        block = ctx.enter_context(nc.Block())
        lt = [lt0, lt1]
        sx = [sx0, sx1]

        @block.sync
        def _(sync):
            sync.dma_start(out=dct[:], in_=dcol).then_inc(ld, 16)
            sync.dma_start(out=mvt[:], in_=mvd).then_inc(ld, 16)
            for k in range(MI):
                if k >= 2:
                    sync.wait_ge(vg, k - 1)
                sl = slice((k % 2) * T, (k % 2) * T + T)
                sync.dma_start(out=yb[:, sl], in_=y0[k * 128:(k + 1) * 128, :]).then_inc(lt[k % 2], 16)
                sync.dma_start(out=xib[:, sl], in_=xi0[k * 128:(k + 1) * 128, :]).then_inc(lt[k % 2], 16)
                sync.dma_start(out=zb[:, sl], in_=z0[k * 128:(k + 1) * 128, :]).then_inc(lt[k % 2], 16)
            for k in range(MI):
                if k >= 1:
                    sync.wait_ge(wc, k)
                sync.dma_start(out=wst[:, :], in_=woutT[k * 128:(k + 1) * 128, :]).then_inc(ldw, 16)
            for q in range(NQ):
                for m in range(KD):
                    xi_ = q * KD + m
                    if xi_ >= 1:
                        sync.wait_ge(ve, xi_)
                    sync.dma_start(out=x0t[:, (xi_ % 2) * QW:(xi_ % 2) * QW + QW],
                                   in_=x0f[m * 128:(m + 1) * 128, q * QW:(q + 1) * QW]
                                   ).then_inc(ldx, 16)

        @block.scalar
        def _(scalar):
            for k in range(MI):
                scalar.wait_ge(lt[k % 2], ld_gk(k))
                scalar.activation(out=szt[:, :], in_=zb[:, (k % 2) * T:(k % 2) * T + T],
                                  func=ACT.Silu).then_inc(sz_, 1)
                if k < MI - 1:
                    scalar.wait_ge(vg, k + 1)

        @block.tensor
        def _(tensor):
            for q in range(NQ):
                for k in range(MI):
                    tensor.wait_ge(vg, k + 1)
                    if k == 0:
                        tensor.wait_ge(wc, MI)
                        if q >= 1:
                            tensor.wait_ge(ve, q * KD)
                    for m in range(KD):
                        ins = tensor.matmul(ps[m][:, :], wob[:, k * D + m * 128:k * D + (m + 1) * 128],
                                            gtb[:, k * T + q * QW:k * T + q * QW + QW],
                                            start=(k == 0), stop=(k == MI - 1))
                    ins.then_inc(mmq, 1)

        @block.vector
        def _(vector):
            vector.wait_ge(ld, 32)
            for k in range(MI):
                vector.wait_ge(lt[k % 2], ld_gk(k))
                vector.wait_ge(sz_, k + 1)
                sl = slice((k % 2) * T, (k % 2) * T + T)
                gsl = slice(k * T, (k + 1) * T)
                vector.scalar_tensor_tensor(out=gtb[:, gsl], in0=xib[:, sl],
                                            scalar=dct[:, k:k + 1], in1=yb[:, sl],
                                            op0=ALU.mult, op1=ALU.add)
                vector.tensor_tensor(out=gtb[:, gsl], in0=gtb[:, gsl], in1=szt[:, :],
                                     op=ALU.mult).then_inc(vg, 1)
            for k in range(MI):
                vector.wait_ge(ldw, ld_w(k))
                vector.tensor_copy(out=wob[:, k * D:(k + 1) * D], in_=wst[:, :]).then_inc(wc, 1)
            for q in range(NQ):
                for m in range(KD):
                    xi_ = q * KD + m
                    vector.wait_ge(mmq, (q + 1) * MI)
                    vector.wait_ge(ldx, (xi_ + 1) * 16)
                    if xi_ >= 2:
                        vector.wait_ge(sx[xi_ % 2], (xi_ // 2) * 16)
                    vector.scalar_tensor_tensor(out=x1t[:, (xi_ % 2) * QW:(xi_ % 2) * QW + QW],
                                                in0=ps[m][:, :], scalar=mvt[:, 16 + m:17 + m],
                                                in1=x0t[:, (xi_ % 2) * QW:(xi_ % 2) * QW + QW],
                                                op0=ALU.mult, op1=ALU.add).then_inc(ve, 1)

        @block.gpsimd
        def _(gpsimd):
            for q in range(NQ):
                for m in range(KD):
                    xi_ = q * KD + m
                    gpsimd.wait_ge(ve, xi_ + 1)
                    gpsimd.dma_start(out=x1o[m * 128:(m + 1) * 128, q * QW:(q + 1) * QW],
                                     in_=x1t[:, (xi_ % 2) * QW:(xi_ % 2) * QW + QW]
                                     ).then_inc(sx[xi_ % 2], 16)
            gpsimd.wait_ge(sx0, NQ * KD * 8)
            gpsimd.wait_ge(sx1, NQ * KD * 8)

    return nc



def build_modln():
    nc = bass.Bass('TRN2', target_bir_lowering=False, detect_race_conditions=False)
    xf = nc.dram_tensor('xf', [D, T], F32, kind='ExternalInput').ap()
    mvd = nc.dram_tensor('mvd', [128, 48], F32, kind='ExternalInput').ap()
    ho = nc.dram_tensor('ho', [D, T], F32, kind='ExternalOutput').ap()
    lmax = nc.dram_tensor('lmax', [128, KD], F32, kind='ExternalOutput').ap()

    nbs = nbl(T)
    NB = len(nbs)

    def ld_stat(k):
        return (1 + k + 1) * 16

    def ld_app(k):
        return (1 + KD + k + 1) * 16

    with ExitStack() as ctx:
        stg = ctx.enter_context(nc.sbuf_tensor('stg', [128, T], F32))
        scr = ctx.enter_context(nc.sbuf_tensor('scr', [128, 4 * T], BF16))
        rows = ctx.enter_context(nc.sbuf_tensor('rows', [1, 4 * T], F32))
        aB = ctx.enter_context(nc.sbuf_tensor('aB', [128, T], F32))
        bB = ctx.enter_context(nc.sbuf_tensor('bB', [128, T], F32))
        asc = ctx.enter_context(nc.sbuf_tensor('asc', [128, T], F32))
        hto = ctx.enter_context(nc.sbuf_tensor('hto', [128, 2 * T], F32))
        mvt = ctx.enter_context(nc.sbuf_tensor('mvt', [128, 48], F32))
        onesb = ctx.enter_context(nc.sbuf_tensor('onesb', [128, 1], BF16))
        onesr = ctx.enter_context(nc.sbuf_tensor('onesr', [1, 128], F32))
        rmx = ctx.enter_context(nc.sbuf_tensor('rmx', [128, KD], F32))
        pa = ctx.enter_context(nc.psum_tensor('pa', [128, 512], F32))
        pb = ctx.enter_context(nc.psum_tensor('pb', [128, 512], F32))
        ld = ctx.enter_context(nc.semaphore('ld'))
        sqm = ctx.enter_context(nc.semaphore('sqm'))
        ms = ctx.enter_context(nc.semaphore('ms'))
        msv = ctx.enter_context(nc.semaphore('msv'))
        sr = ctx.enter_context(nc.semaphore('sr'))
        vr = ctx.enter_context(nc.semaphore('vr'))
        sha = ctx.enter_context(nc.semaphore('sha'))
        shb = ctx.enter_context(nc.semaphore('shb'))
        slm = ctx.enter_context(nc.semaphore('slm'))
        vfl = ctx.enter_context(nc.semaphore('vfl'))
        mbc = ctx.enter_context(nc.semaphore('mbc'))
        sbv = ctx.enter_context(nc.semaphore('sbv'))
        block = ctx.enter_context(nc.Block())
        sh = [sha, shb]

        def R(r):
            return rows[0:1, r * T:(r + 1) * T]

        def Rs(r, o, n):
            return rows[0:1, r * T + o:r * T + o + n]

        @block.sync
        def _(sync):
            sync.dma_start(out=mvt[:], in_=mvd).then_inc(ld, 16)
            for k in range(KD):
                if k >= 1:
                    sync.wait_ge(sqm, k)
                sync.dma_start(out=stg[:, :], in_=xf[k * 128:(k + 1) * 128, :]).then_inc(ld, 16)
            for k in range(KD):
                sync.wait_ge(sqm, KD)
                if k >= 1:
                    sync.wait_ge(vr, 3 + k)
                sync.dma_start(out=stg[:, :], in_=xf[k * 128:(k + 1) * 128, :]).then_inc(ld, 16)

        @block.scalar
        def _(scalar):
            for k in range(KD):
                scalar.wait_ge(ld, ld_stat(k))
                if k >= 2:
                    scalar.wait_ge(ms, (k - 1) * 2 * NB)
                scalar.activation(out=scr[:, (k % 2) * T:(k % 2) * T + T], in_=stg[:, :],
                                  func=ACT.Copy)
                scalar.activation(out=scr[:, (2 + k % 2) * T:(2 + k % 2) * T + T],
                                  in_=stg[:, :], func=ACT.Square).then_inc(sqm, 1)
            scalar.wait_ge(vr, 2)
            scalar.activation(out=R(2), in_=R(2), func=ACT.Ln)
            scalar.activation(out=R(2), in_=R(2), func=ACT.Exp, scale=-0.5).then_inc(sr, 1)
            for bi, (o, n) in enumerate(nbs):
                for ri in range(2):
                    j = bi * 2 + ri
                    scalar.wait_ge(mbc, j + 1)
                    dst = (aB if ri == 0 else bB)[:, o:o + n]
                    scalar.activation(out=dst, in_=(pa if j % 2 == 0 else pb)[:, :n],
                                      func=ACT.Copy).then_inc(sbv, 1)

        @block.tensor
        def _(tensor):
            tensor.wait_ge(vr, 1)
            for k in range(KD):
                tensor.wait_ge(sqm, k + 1)
                for bi, (o, n) in enumerate(nbs):
                    for wi in range(2):
                        smi = k * 2 * NB + bi * 2 + wi
                        if smi >= 2:
                            tensor.wait_ge(msv, smi - 1)
                        pdst = pa if smi % 2 == 0 else pb
                        base = (0 if wi == 0 else 2) + k % 2
                        ins = tensor.matmul(pdst[:1, :n], onesb[:, 0:1],
                                            scr[:, base * T + o:base * T + o + n],
                                            start=True, stop=True)
                        ins.then_inc(ms, 1)
            tensor.wait_ge(vr, 3)
            tensor.wait_ge(msv, KD * 2 * NB)
            for bi, (o, n) in enumerate(nbs):
                for ri in range(2):
                    j = bi * 2 + ri
                    if j >= 2:
                        tensor.wait_ge(sbv, j - 1)
                    tensor.matmul((pa if j % 2 == 0 else pb)[:, :n], onesr[0:1, :],
                                  Rs(2 + ri, o, n), start=True,
                                  stop=True).then_inc(mbc, 1)

        @block.vector
        def _(vector):
            vfl_c = [0]

            def vsync(ins):
                vfl_c[0] += 1
                ins.then_inc(vfl, 1)
                vector.wait_ge(vfl, vfl_c[0])

            vector.wait_ge(ld, 16)
            vector.memset(onesb[:], 1.0)
            vector.memset(onesr[:], 1.0)
            vector.memset(rmx[:], 0.0)
            vector.engine_nop().then_inc(vr, 1)
            for k in range(KD):
                for bi, (o, n) in enumerate(nbs):
                    for wi in range(2):
                        smi = k * 2 * NB + bi * 2 + wi
                        vector.wait_ge(ms, smi + 1)
                        srcp = pa if smi % 2 == 0 else pb
                        r = wi
                        if k == 0:
                            vector.tensor_copy(out=Rs(r, o, n), in_=srcp[0:1, :n]).then_inc(msv, 1)
                        else:
                            vector.tensor_tensor(out=Rs(r, o, n), in0=Rs(r, o, n),
                                                 in1=srcp[0:1, :n], op=ALU.add).then_inc(msv, 1)
            vector.tensor_scalar(out=R(0), in0=R(0), scalar1=1.0 / D, scalar2=None, op0=ALU.mult)
            vector.tensor_scalar(out=R(1), in0=R(1), scalar1=1.0 / D, scalar2=None, op0=ALU.mult)
            vector.tensor_tensor(out=R(2), in0=R(0), in1=R(0), op=ALU.mult)
            vector.tensor_tensor(out=R(2), in0=R(1), in1=R(2), op=ALU.subtract)
            vector.tensor_scalar(out=R(2), in0=R(2), scalar1=EPS, scalar2=None,
                                 op0=ALU.add).then_inc(vr, 1)
            vector.wait_ge(sr, 1)
            vector.tensor_tensor(out=R(3), in0=R(0), in1=R(2), op=ALU.mult).then_inc(vr, 1)
            for k in range(KD):
                vector.wait_ge(ld, ld_app(k))
                if k == 0:
                    vector.wait_ge(sbv, 2 * NB)
                if k >= 2:
                    vector.wait_ge(sh[k % 2], (k // 2) * 16)
                sl = slice((k % 2) * T, (k % 2) * T + T)
                vector.tensor_tensor(out=asc[:, :], in0=stg[:, :], in1=aB[:], op=ALU.mult)
                vector.tensor_tensor(out=asc[:, :], in0=asc[:, :], in1=bB[:], op=ALU.subtract)
                vector.tensor_scalar(out=hto[:, sl], in0=asc[:, :],
                                     scalar1=mvt[:, 32 + k:33 + k], scalar2=mvt[:, 24 + k:25 + k],
                                     op0=ALU.mult, op1=ALU.add).then_inc(vr, 1)
                vector.reduce_max(out=rmx[:, k:k + 1], in_=hto[:, sl], axis=AX.X,
                                  apply_absolute_value=True)
            vector.engine_nop().then_inc(vr, 1)

        @block.gpsimd
        def _(gpsimd):
            for k in range(KD):
                gpsimd.wait_ge(vr, 4 + k)
                gpsimd.dma_start(out=ho[k * 128:(k + 1) * 128, :],
                                 in_=hto[:, (k % 2) * T:(k % 2) * T + T]).then_inc(sh[k % 2], 16)
            gpsimd.wait_ge(vr, 4 + KD)
            gpsimd.dma_start(out=lmax, in_=rmx[:]).then_inc(slm, 16)
            gpsimd.wait_ge(sha, KD * 8)
            gpsimd.wait_ge(shb, KD * 8)
            gpsimd.wait_ge(slm, 16)

    return nc



def build_ffn0a():
    nc = bass.Bass('TRN2', target_bir_lowering=False, detect_race_conditions=False)
    h1f = nc.dram_tensor('h1f', [D, T], F32, kind='ExternalInput').ap()
    gmaxa = nc.dram_tensor('gmaxa', [1, NC * D], F32, kind='ExternalInput').ap()
    clipx = nc.dram_tensor('clipx', [1, 32], F32, kind='ExternalInput').ap()
    wg = nc.dram_tensor('wg', [FF, D], F32, kind='ExternalInput').ap()
    wu = nc.dram_tensor('wu', [FF, D], F32, kind='ExternalInput').ap()
    ident = nc.dram_tensor('ident', [128, 128], BF16, kind='ExternalInput').ap()
    h2o = nc.dram_tensor('h2o', [FF, T], F32, kind='ExternalOutput').ap()
    rm2o = nc.dram_tensor('rm2o', [128, MF], F32, kind='ExternalOutput').ap()

    nbs = nbl(T)
    NW = 2 * MF
    GBC = 8
    GW = NC * D // GBC

    def ld_row1(li):
        return (li // 2 + 1) * 16

    def ld_row2(li):
        return (NW // 2 + li // 2 + 1) * 16

    def ld_h1(k):
        return (k // 2 + 1) * 16

    with ExitStack() as ctx:
        wrow = ctx.enter_context(nc.sbuf_tensor('wrow', [128, 2 * D], F32))
        wqrow = ctx.enter_context(nc.sbuf_tensor('wqrow', [128, 2 * D], BF16))
        sgc = ctx.enter_context(nc.sbuf_tensor('sgc', [128, NW], F32))
        rcolt = ctx.enter_context(nc.sbuf_tensor('rcolt', [128, NW], F32))
        dq = ctx.enter_context(nc.sbuf_tensor('dq', [128, NW], F32))
        wqg = ctx.enter_context(nc.sbuf_tensor('wqg', [128, KD * FF], BF16))
        wqu = ctx.enter_context(nc.sbuf_tensor('wqu', [128, KD * FF], BF16))
        xqt = ctx.enter_context(nc.sbuf_tensor('xqt', [128, KD * T], BF16))
        h1st = ctx.enter_context(nc.sbuf_tensor('h1st', [128, 2 * T], F32))
        asc = ctx.enter_context(nc.sbuf_tensor('asc', [128, T], F32))
        idt = ctx.enter_context(nc.sbuf_tensor('idt', [128, 128], BF16))
        onesr = ctx.enter_context(nc.sbuf_tensor('onesr', [1, 128], F32))
        srow = ctx.enter_context(nc.sbuf_tensor('srow', [1, 32], F32))
        scol = ctx.enter_context(nc.sbuf_tensor('scol', [128, 2], F32))
        sgt = ctx.enter_context(nc.sbuf_tensor('sgt', [128, 512], F32))
        h2t = ctx.enter_context(nc.sbuf_tensor('h2t', [128, 2 * 512], F32))
        rm2 = ctx.enter_context(nc.sbuf_tensor('rm2', [128, MF], F32))
        rmt = ctx.enter_context(nc.sbuf_tensor('rmt', [128, 1], F32))
        psA0 = ctx.enter_context(nc.psum_tensor('psA0', [128, 512], F32))
        psA1 = ctx.enter_context(nc.psum_tensor('psA1', [128, 512], F32))
        psB0 = ctx.enter_context(nc.psum_tensor('psB0', [128, 512], F32))
        psB1 = ctx.enter_context(nc.psum_tensor('psB1', [128, 512], F32))
        ld = ctx.enter_context(nc.semaphore('ld'))
        ldg0 = ctx.enter_context(nc.semaphore('ldg0'))
        ldg1 = ctx.enter_context(nc.semaphore('ldg1'))
        lw0 = ctx.enter_context(nc.semaphore('lw0'))
        lw1 = ctx.enter_context(nc.semaphore('lw1'))
        lh0 = ctx.enter_context(nc.semaphore('lh0'))
        lh1 = ctx.enter_context(nc.semaphore('lh1'))
        sr = ctx.enter_context(nc.semaphore('sr'))
        vgm = ctx.enter_context(nc.semaphore('vgm'))
        vgc = ctx.enter_context(nc.semaphore('vgc'))
        mbc = ctx.enter_context(nc.semaphore('mbc'))
        vfl = ctx.enter_context(nc.semaphore('vfl'))
        vr1 = ctx.enter_context(nc.semaphore('vr1'))
        vq = ctx.enter_context(nc.semaphore('vq'))
        vrh = ctx.enter_context(nc.semaphore('vrh'))
        vmk = ctx.enter_context(nc.semaphore('vmk'))
        mmT = ctx.enter_context(nc.semaphore('mmT'))
        sevT = ctx.enter_context(nc.semaphore('sevT'))
        mmg = ctx.enter_context(nc.semaphore('mmg'))
        seva = ctx.enter_context(nc.semaphore('seva'))
        vev = ctx.enter_context(nc.semaphore('vev'))
        sh2a = ctx.enter_context(nc.semaphore('sh2a'))
        sh2b = ctx.enter_context(nc.semaphore('sh2b'))
        slr = ctx.enter_context(nc.semaphore('slr'))
        block = ctx.enter_context(nc.Block())

        psA = [psA0, psA1]
        psB = [psB0, psB1]
        ldg = [ldg0, ldg1]
        lw = [lw0, lw1]
        lh = [lh0, lh1]
        sh2 = [sh2a, sh2b]

        @block.sync
        def _(sync):
            sync.dma_start(out=srow[:, :], in_=clipx).then_inc(ld, 16)
            for c in range(GBC):
                if c >= 2:
                    sync.wait_ge(vgc, c - 1)
                sync.dma_start(out=asc[0:1, (c % 2) * GW:(c % 2) * GW + GW],
                               in_=gmaxa[0:1, c * GW:(c + 1) * GW]).then_inc(ldg[c % 2], 16)
            sync.dma_start(out=idt[:], in_=ident).then_inc(ld, 16)
            for wi, w in enumerate((wg, wu)):
                for m in range(MF):
                    li = wi * MF + m
                    if li >= 2:
                        sync.wait_ge(vr1, li - 1)
                    sync.dma_start(out=wrow[:, (li % 2) * D:(li % 2) * D + D],
                                   in_=w[m * 128:(m + 1) * 128, :]).then_inc(lw[li % 2], 16)
            for wi, w in enumerate((wg, wu)):
                for m in range(MF):
                    li = wi * MF + m
                    if li < 2:
                        sync.wait_ge(vr1, NW - 2 + li + 1)
                    else:
                        sync.wait_ge(vq, li - 1)
                    sync.dma_start(out=wrow[:, (li % 2) * D:(li % 2) * D + D],
                                   in_=w[m * 128:(m + 1) * 128, :]).then_inc(lw[li % 2], 16)
            for k in range(KD):
                if k >= 2:
                    sync.wait_ge(vrh, k - 1)
                sync.dma_start(out=h1st[:, (k % 2) * T:(k % 2) * T + T],
                               in_=h1f[k * 128:(k + 1) * 128, :]).then_inc(lh[k % 2], 16)

        @block.scalar
        def _(scalar):
            scalar.wait_ge(ld, 32)
            scalar.activation(out=srow[0:1, 1:2], in_=srow[0:1, 0:1],
                              func=ACT.Sigmoid).then_inc(sr, 1)
            wqgv = wqg[:, :].rearrange("p (k f) -> p k f", f=FF)
            wquv = wqu[:, :].rearrange("p (k f) -> p k f", f=FF)
            for li in range(NW):
                wi, m = divmod(li, MF)
                dstv = wqgv if wi == 0 else wquv
                scalar.wait_ge(mmT, li + 1)
                scalar.activation(out=dstv[:, 0:4, m * 128:(m + 1) * 128],
                                  in_=psA[li % 2][:, :].rearrange("p (k f) -> p k f", f=128),
                                  func=ACT.Copy).then_inc(sevT, 1)
                scalar.activation(out=dstv[:, 4:8, m * 128:(m + 1) * 128],
                                  in_=psB[li % 2][:, :].rearrange("p (k f) -> p k f", f=128),
                                  func=ACT.Copy).then_inc(sevT, 1)
            for m in range(MF):
                for nb in range(len(nbs)):
                    ei = m * 4 + nb
                    scalar.wait_ge(mmg, 2 * ei + 1)
                    if ei >= 1:
                        scalar.wait_ge(vev, ei)
                    scalar.activation(out=sgt[:, :], in_=psA[ei % 2][:, :],
                                      func=ACT.Silu,
                                      scale=dq[:, m:m + 1]).then_inc(seva, 1)

        @block.tensor
        def _(tensor):
            tensor.wait_ge(ld, 32)
            tensor.wait_ge(vgm, 1)
            tensor.matmul(psB1[:, 0:2], onesr[0:1, :], srow[0:1, 4:6],
                          start=True, stop=True).then_inc(mbc, 1)
            for li in range(NW):
                tensor.wait_ge(vq, li + 1)
                if li >= 2:
                    tensor.wait_ge(sevT, 2 * (li - 1))
                base = (li % 2) * D
                for j in range(4):
                    tensor.matmul(psA[li % 2][:, j * 128:(j + 1) * 128],
                                  wqrow[:, base + j * 128:base + (j + 1) * 128],
                                  idt[:, :], start=True, stop=True)
                for j in range(4, 8):
                    ins = tensor.matmul(psB[li % 2][:, (j - 4) * 128:(j - 3) * 128],
                                        wqrow[:, base + j * 128:base + (j + 1) * 128],
                                        idt[:, :], start=True, stop=True)
                ins.then_inc(mmT, 1)
            tensor.wait_ge(vmk, 1)
            tensor.wait_ge(sevT, 2 * NW)
            for m in range(MF):
                for nb, (o, n) in enumerate(nbs):
                    ei = m * 4 + nb
                    if ei >= 2:
                        tensor.wait_ge(seva, ei - 1)
                        tensor.wait_ge(vev, ei - 1)
                    for k in range(KD):
                        ins = tensor.matmul(psA[ei % 2][:, :n],
                                            wqg[:, k * FF + m * 128:k * FF + (m + 1) * 128],
                                            xqt[:, k * T + o:k * T + o + n],
                                            start=(k == 0), stop=(k == KD - 1))
                    ins.then_inc(mmg, 1)
                    for k in range(KD):
                        ins = tensor.matmul(psB[ei % 2][:, :n],
                                            wqu[:, k * FF + m * 128:k * FF + (m + 1) * 128],
                                            xqt[:, k * T + o:k * T + o + n],
                                            start=(k == 0), stop=(k == KD - 1))
                    ins.then_inc(mmg, 1)

        @block.vector
        def _(vector):
            vfl_c = [0]

            def vsync(ins):
                vfl_c[0] += 1
                ins.then_inc(vfl, 1)
                vector.wait_ge(vfl, vfl_c[0])

            vector.memset(onesr[:], 1.0)
            for c in range(GBC):
                vector.wait_ge(ldg[c % 2], (c // 2 + 1) * 16)
                half = asc[0:1, (c % 2) * GW:(c % 2) * GW + GW]
                if c == 0:
                    vsync(vector.reduce_max(out=srow[0:1, 2:3], in_=half, axis=AX.X,
                                            apply_absolute_value=True))
                else:
                    vsync(vector.reduce_max(out=srow[0:1, 6:7], in_=half, axis=AX.X,
                                            apply_absolute_value=True))
                    vsync(vector.tensor_tensor(out=srow[0:1, 2:3], in0=srow[0:1, 2:3],
                                               in1=srow[0:1, 6:7], op=ALU.max))
                vector.engine_nop().then_inc(vgc, 1)
            vector.wait_ge(sr, 1)
            vsync(vector.tensor_tensor(out=srow[0:1, 3:4], in0=srow[0:1, 1:2],
                                       in1=srow[0:1, 2:3], op=ALU.mult))
            vsync(vector.tensor_scalar(out=srow[0:1, 3:4], in0=srow[0:1, 3:4],
                                       scalar1=1.0 / QMAX, scalar2=None, op0=ALU.mult))
            vsync(vector.reciprocal(out=srow[0:1, 4:5], in_=srow[0:1, 3:4]))
            vsync(vector.tensor_scalar(out=srow[0:1, 5:6], in0=srow[0:1, 3:4],
                                       scalar1=1.0 / QMAX, scalar2=None, op0=ALU.mult))
            vector.engine_nop().then_inc(vgm, 1)
            vector.wait_ge(mbc, 1)
            vsync(vector.tensor_copy(out=scol[:, 0:2], in_=psB1[:, 0:2]))
            for li in range(NW):
                vector.wait_ge(lw[li % 2], ld_row1(li))
                vector.reduce_max(out=sgc[:, li:li + 1],
                                  in_=wrow[:, (li % 2) * D:(li % 2) * D + D],
                                  axis=AX.X, apply_absolute_value=True).then_inc(vr1, 1)
            vsync(vector.engine_nop())
            vsync(vector.reciprocal(out=rcolt[:, :], in_=sgc[:, :]))
            vsync(vector.tensor_scalar(out=rcolt[:, :], in0=rcolt[:, :],
                                       scalar1=QMAX, scalar2=None, op0=ALU.mult))
            vsync(vector.tensor_scalar(out=dq[:, :], in0=sgc[:, :],
                                       scalar1=scol[:, 1:2], scalar2=None, op0=ALU.mult))
            for li in range(NW):
                vector.wait_ge(lw[li % 2], ld_row2(li))
                if li >= 2:
                    vector.wait_ge(mmT, li - 1)
                sl = slice((li % 2) * D, (li % 2) * D + D)
                vector.tensor_scalar(out=asc[:, 0:D], in0=wrow[:, sl],
                                     scalar1=rcolt[:, li:li + 1], scalar2=MAGIC,
                                     op0=ALU.mult, op1=ALU.add)
                vector.tensor_scalar(out=wqrow[:, sl], in0=asc[:, 0:D],
                                     scalar1=MAGIC, scalar2=None,
                                     op0=ALU.subtract).then_inc(vq, 1)
            for k in range(KD):
                vector.wait_ge(lh[k % 2], ld_h1(k))
                sl = h1st[:, (k % 2) * T:(k % 2) * T + T]
                vector.tensor_scalar(out=asc[:, 0:T], in0=sl, scalar1=scol[:, 0:1],
                                     scalar2=MAGIC, op0=ALU.mult, op1=ALU.add)
                vector.tensor_scalar(out=asc[:, 0:T], in0=asc[:, 0:T],
                                     scalar1=MAGIC, scalar2=127.0,
                                     op0=ALU.subtract, op1=ALU.min)
                vector.tensor_scalar(out=xqt[:, k * T:(k + 1) * T], in0=asc[:, 0:T],
                                     scalar1=-128.0, scalar2=None,
                                     op0=ALU.max).then_inc(vrh, 1)
            vector.memset(rm2[:], 0.0)
            vector.engine_nop().then_inc(vmk, 1)
            for m in range(MF):
                for nb, (o, n) in enumerate(nbs):
                    ei = m * 4 + nb
                    vector.wait_ge(mmg, 2 * ei + 2)
                    vector.wait_ge(seva, ei + 1)
                    if ei >= 2:
                        vector.wait_ge(sh2[ei % 2], (ei // 2) * 16)
                    vector.scalar_tensor_tensor(out=h2t[:, (ei % 2) * 512:(ei % 2) * 512 + n],
                                                in0=psB[ei % 2][:, :n],
                                                scalar=dq[:, MF + m:MF + m + 1],
                                                in1=sgt[:, :n], op0=ALU.mult, op1=ALU.mult)
                    vsync(vector.reduce_max(out=rmt[:, 0:1],
                                            in_=h2t[:, (ei % 2) * 512:(ei % 2) * 512 + n],
                                            axis=AX.X, apply_absolute_value=True))
                    vsync(vector.tensor_tensor(out=rm2[:, m:m + 1], in0=rm2[:, m:m + 1],
                                               in1=rmt[:, 0:1], op=ALU.max))
                    vector.engine_nop().then_inc(vev, 1)

        @block.gpsimd
        def _(gpsimd):
            for m in range(MF):
                for nb, (o, n) in enumerate(nbs):
                    ei = m * 4 + nb
                    gpsimd.wait_ge(vev, ei + 1)
                    gpsimd.dma_start(out=h2o[m * 128:(m + 1) * 128, o:o + n],
                                     in_=h2t[:, (ei % 2) * 512:(ei % 2) * 512 + n]
                                     ).then_inc(sh2[ei % 2], 16)
            gpsimd.wait_ge(vev, MF * 4)
            gpsimd.dma_start(out=rm2o, in_=rm2[:, :]).then_inc(slr, 16)
            gpsimd.wait_ge(sh2a, MF * 2 * 16)
            gpsimd.wait_ge(sh2b, MF * 2 * 16)
            gpsimd.wait_ge(slr, 16)

    return nc



def build_ffn0b():
    nc = bass.Bass('TRN2', target_bir_lowering=False, detect_race_conditions=False)
    h2f = nc.dram_tensor('h2f', [FF, T], F32, kind='ExternalInput').ap()
    gmaxa = nc.dram_tensor('gmaxa', [1, NC * FF], F32, kind='ExternalInput').ap()
    clipd = nc.dram_tensor('clipd', [1, 32], F32, kind='ExternalInput').ap()
    wd = nc.dram_tensor('wd', [D, FF], F32, kind='ExternalInput').ap()
    x1f = nc.dram_tensor('x1f', [D, T], F32, kind='ExternalInput').ap()
    mvd = nc.dram_tensor('mvd', [128, 48], F32, kind='ExternalInput').ap()
    ident = nc.dram_tensor('ident', [128, 128], BF16, kind='ExternalInput').ap()
    x2o = nc.dram_tensor('x2o', [D, T], F32, kind='ExternalOutput').ap()
    if _FFB_DEBUG:
        xqo = nc.dram_tensor('xqo', [MF * 128, T], BF16, kind='ExternalOutput').ap()
        wdqo = nc.dram_tensor('wdqo', [128, MF * D], BF16, kind='ExternalOutput').ap()

    nbs = nbl(T)
    TGP = [(0, 4), (4, 4), (8, 4), (12, 4), (16, 4), (20, 2)]
    NG = len(TGP)
    GBC = 16
    GW = NC * FF // GBC

    def ld_row1(m):
        return (m + 1) * 16

    def ld_row2(m):
        return (KD + m + 1) * 16

    def ld_h2(k):
        return (k + 1) * 16

    def ld_x1(ei):
        return (ei // 2 + 1) * 16

    with ExitStack() as ctx:
        wrow = ctx.enter_context(nc.sbuf_tensor('wrow', [128, FF], F32))
        wqrow = ctx.enter_context(nc.sbuf_tensor('wqrow', [128, FF], BF16))
        sdc = ctx.enter_context(nc.sbuf_tensor('sdc', [128, KD], F32))
        rcolt = ctx.enter_context(nc.sbuf_tensor('rcolt', [128, KD], F32))
        coef = ctx.enter_context(nc.sbuf_tensor('coef', [128, KD], F32))
        wdq = ctx.enter_context(nc.sbuf_tensor('wdq', [128, MF * D], BF16))
        xqt = ctx.enter_context(nc.sbuf_tensor('xqt', [128, MF * T], BF16))
        h2st = ctx.enter_context(nc.sbuf_tensor('h2st', [128, T], F32))
        asc = ctx.enter_context(nc.sbuf_tensor('asc', [128, FF], F32))
        idt = ctx.enter_context(nc.sbuf_tensor('idt', [128, 128], BF16))
        onesr = ctx.enter_context(nc.sbuf_tensor('onesr', [1, 128], F32))
        srow = ctx.enter_context(nc.sbuf_tensor('srow', [1, 32], F32))
        scol = ctx.enter_context(nc.sbuf_tensor('scol', [128, 2], F32))
        mvt = ctx.enter_context(nc.sbuf_tensor('mvt', [128, 48], F32))
        x1t = ctx.enter_context(nc.sbuf_tensor('x1t', [128, 2 * 512], F32))
        x2t = ctx.enter_context(nc.sbuf_tensor('x2t', [128, 2 * 512], F32))
        ps0 = ctx.enter_context(nc.psum_tensor('ps0', [128, 512], F32))
        ps1 = ctx.enter_context(nc.psum_tensor('ps1', [128, 512], F32))
        ps2 = ctx.enter_context(nc.psum_tensor('ps2', [128, 512], F32))
        ps3 = ctx.enter_context(nc.psum_tensor('ps3', [128, 512], F32))
        ld = ctx.enter_context(nc.semaphore('ld'))
        ldg0 = ctx.enter_context(nc.semaphore('ldg0'))
        ldg1 = ctx.enter_context(nc.semaphore('ldg1'))
        ldw = ctx.enter_context(nc.semaphore('ldw'))
        ldh = ctx.enter_context(nc.semaphore('ldh'))
        lx0 = ctx.enter_context(nc.semaphore('lx0'))
        lx1 = ctx.enter_context(nc.semaphore('lx1'))
        sr = ctx.enter_context(nc.semaphore('sr'))
        vgm = ctx.enter_context(nc.semaphore('vgm'))
        vgc = ctx.enter_context(nc.semaphore('vgc'))
        mbc = ctx.enter_context(nc.semaphore('mbc'))
        vfl = ctx.enter_context(nc.semaphore('vfl'))
        vr1 = ctx.enter_context(nc.semaphore('vr1'))
        vq = ctx.enter_context(nc.semaphore('vq'))
        vrh = ctx.enter_context(nc.semaphore('vrh'))
        vmk = ctx.enter_context(nc.semaphore('vmk'))
        mmT = ctx.enter_context(nc.semaphore('mmT'))
        sevT = ctx.enter_context(nc.semaphore('sevT'))
        mmg = ctx.enter_context(nc.semaphore('mmg'))
        vev = ctx.enter_context(nc.semaphore('vev'))
        sx2a = ctx.enter_context(nc.semaphore('sx2a'))
        sx2b = ctx.enter_context(nc.semaphore('sx2b'))
        block = ctx.enter_context(nc.Block())

        psum = [ps0, ps1, ps2, ps3]
        ldg = [ldg0, ldg1]
        lx = [lx0, lx1]
        sx2 = [sx2a, sx2b]

        @block.sync
        def _(sync):
            sync.dma_start(out=srow[:, :], in_=clipd).then_inc(ld, 16)
            for c in range(GBC):
                if c >= 2:
                    sync.wait_ge(vgc, c - 1)
                sync.dma_start(out=asc[0:1, (c % 2) * GW:(c % 2) * GW + GW],
                               in_=gmaxa[0:1, c * GW:(c + 1) * GW]).then_inc(ldg[c % 2], 16)
            sync.dma_start(out=mvt[:], in_=mvd).then_inc(ld, 16)
            sync.dma_start(out=idt[:], in_=ident).then_inc(ld, 16)
            for m in range(KD):
                if m >= 1:
                    sync.wait_ge(vr1, m)
                sync.dma_start(out=wrow[:, :],
                               in_=wd[m * 128:(m + 1) * 128, :]).then_inc(ldw, 16)
            for m in range(KD):
                if m == 0:
                    sync.wait_ge(vr1, KD)
                else:
                    sync.wait_ge(vq, m)
                sync.dma_start(out=wrow[:, :],
                               in_=wd[m * 128:(m + 1) * 128, :]).then_inc(ldw, 16)
            for k in range(MF):
                if k >= 1:
                    sync.wait_ge(vrh, k)
                sync.dma_start(out=h2st[:, :],
                               in_=h2f[k * 128:(k + 1) * 128, :]).then_inc(ldh, 16)
            for m in range(KD):
                for nb, (o, n) in enumerate(nbs):
                    ei = m * 4 + nb
                    if ei >= 2:
                        sync.wait_ge(vev, ei - 1)
                    sync.dma_start(out=x1t[:, (ei % 2) * 512:(ei % 2) * 512 + n],
                                   in_=x1f[m * 128:(m + 1) * 128, o:o + n]).then_inc(lx[ei % 2], 16)

        @block.scalar
        def _(scalar):
            scalar.wait_ge(ld, 48)
            scalar.activation(out=srow[0:1, 1:2], in_=srow[0:1, 0:1],
                              func=ACT.Sigmoid).then_inc(sr, 1)
            wdqv = wdq[:, :].rearrange("p (k f) -> p k f", f=D)
            for m in range(KD):
                for gi, (j0, nj) in enumerate(TGP):
                    g = m * NG + gi
                    scalar.wait_ge(mmT, g + 1)
                    scalar.activation(out=wdqv[:, j0:j0 + nj, m * 128:(m + 1) * 128],
                                      in_=psum[g % 2][:, 0:nj * 128].rearrange(
                                          "p (k f) -> p k f", f=128),
                                      func=ACT.Copy).then_inc(sevT, 1)

        @block.tensor
        def _(tensor):
            tensor.wait_ge(ld, 48)
            tensor.wait_ge(vgm, 1)
            tensor.matmul(ps3[:, 0:2], onesr[0:1, :], srow[0:1, 4:6],
                          start=True, stop=True).then_inc(mbc, 1)
            for m in range(KD):
                tensor.wait_ge(vq, m + 1)
                for gi, (j0, nj) in enumerate(TGP):
                    g = m * NG + gi
                    if g >= 2:
                        tensor.wait_ge(sevT, g - 1)
                    for j in range(j0, j0 + nj):
                        ins = tensor.matmul(psum[g % 2][:, (j - j0) * 128:(j - j0 + 1) * 128],
                                            wqrow[:, j * 128:(j + 1) * 128],
                                            idt[:, :], start=True, stop=True)
                    ins.then_inc(mmT, 1)
            tensor.wait_ge(vmk, 1)
            tensor.wait_ge(sevT, KD * NG)
            for m in range(KD):
                for nb, (o, n) in enumerate(nbs):
                    ei = m * 4 + nb
                    if ei >= 4:
                        tensor.wait_ge(vev, ei - 3)
                    for k in range(MF):
                        ins = tensor.matmul(psum[ei % 4][:, :n],
                                            wdq[:, k * D + m * 128:k * D + (m + 1) * 128],
                                            xqt[:, k * T + o:k * T + o + n],
                                            start=(k == 0), stop=(k == MF - 1))
                    ins.then_inc(mmg, 1)

        @block.vector
        def _(vector):
            vfl_c = [0]

            def vsync(ins):
                vfl_c[0] += 1
                ins.then_inc(vfl, 1)
                vector.wait_ge(vfl, vfl_c[0])

            vector.memset(onesr[:], 1.0)
            for c in range(GBC):
                vector.wait_ge(ldg[c % 2], (c // 2 + 1) * 16)
                half = asc[0:1, (c % 2) * GW:(c % 2) * GW + GW]
                if c == 0:
                    vsync(vector.reduce_max(out=srow[0:1, 2:3], in_=half, axis=AX.X,
                                            apply_absolute_value=True))
                else:
                    vsync(vector.reduce_max(out=srow[0:1, 6:7], in_=half, axis=AX.X,
                                            apply_absolute_value=True))
                    vsync(vector.tensor_tensor(out=srow[0:1, 2:3], in0=srow[0:1, 2:3],
                                               in1=srow[0:1, 6:7], op=ALU.max))
                vector.engine_nop().then_inc(vgc, 1)
            vector.wait_ge(sr, 1)
            vsync(vector.tensor_tensor(out=srow[0:1, 3:4], in0=srow[0:1, 1:2],
                                       in1=srow[0:1, 2:3], op=ALU.mult))
            vsync(vector.tensor_scalar(out=srow[0:1, 3:4], in0=srow[0:1, 3:4],
                                       scalar1=1.0 / QMAX, scalar2=None, op0=ALU.mult))
            vsync(vector.reciprocal(out=srow[0:1, 4:5], in_=srow[0:1, 3:4]))
            vsync(vector.tensor_scalar(out=srow[0:1, 5:6], in0=srow[0:1, 3:4],
                                       scalar1=1.0 / QMAX, scalar2=None, op0=ALU.mult))
            vector.engine_nop().then_inc(vgm, 1)
            vector.wait_ge(ld, 48)
            vector.wait_ge(mbc, 1)
            vsync(vector.tensor_copy(out=scol[:, 0:2], in_=ps3[:, 0:2]))
            for m in range(KD):
                vector.wait_ge(ldw, ld_row1(m))
                vector.reduce_max(out=sdc[:, m:m + 1], in_=wrow[:, :],
                                  axis=AX.X, apply_absolute_value=True).then_inc(vr1, 1)
            vsync(vector.engine_nop())
            vsync(vector.reciprocal(out=rcolt[:, :], in_=sdc[:, :]))
            vsync(vector.tensor_scalar(out=rcolt[:, :], in0=rcolt[:, :],
                                       scalar1=QMAX, scalar2=None, op0=ALU.mult))
            vsync(vector.tensor_scalar(out=coef[:, :], in0=sdc[:, :],
                                       scalar1=scol[:, 1:2], scalar2=None, op0=ALU.mult))
            vsync(vector.tensor_tensor(out=coef[:, :], in0=coef[:, :],
                                       in1=mvt[:, 40:48], op=ALU.mult))
            for m in range(KD):
                vector.wait_ge(ldw, ld_row2(m))
                if m >= 1:
                    vector.wait_ge(mmT, m * NG)
                vector.tensor_scalar(out=asc[:, :], in0=wrow[:, :],
                                     scalar1=rcolt[:, m:m + 1], scalar2=MAGIC,
                                     op0=ALU.mult, op1=ALU.add)
                vector.tensor_scalar(out=wqrow[:, :], in0=asc[:, :],
                                     scalar1=MAGIC, scalar2=None,
                                     op0=ALU.subtract).then_inc(vq, 1)
            for k in range(MF):
                vector.wait_ge(ldh, ld_h2(k))
                vector.tensor_scalar(out=asc[:, 0:T], in0=h2st[:, :], scalar1=scol[:, 0:1],
                                     scalar2=MAGIC, op0=ALU.mult, op1=ALU.add)
                vector.tensor_scalar(out=asc[:, 0:T], in0=asc[:, 0:T],
                                     scalar1=MAGIC, scalar2=127.0,
                                     op0=ALU.subtract, op1=ALU.min)
                vector.tensor_scalar(out=xqt[:, k * T:(k + 1) * T], in0=asc[:, 0:T],
                                     scalar1=-128.0, scalar2=None,
                                     op0=ALU.max).then_inc(vrh, 1)
            vector.engine_nop().then_inc(vmk, 1)
            for m in range(KD):
                for nb, (o, n) in enumerate(nbs):
                    ei = m * 4 + nb
                    vector.wait_ge(mmg, ei + 1)
                    vector.wait_ge(lx[ei % 2], ld_x1(ei))
                    if ei >= 2:
                        vector.wait_ge(sx2[ei % 2], (ei // 2) * 16)
                    vector.scalar_tensor_tensor(out=x2t[:, (ei % 2) * 512:(ei % 2) * 512 + n],
                                                in0=psum[ei % 4][:, :n], scalar=coef[:, m:m + 1],
                                                in1=x1t[:, (ei % 2) * 512:(ei % 2) * 512 + n],
                                                op0=ALU.mult, op1=ALU.add).then_inc(vev, 1)

        @block.gpsimd
        def _(gpsimd):
            for m in range(KD):
                for nb, (o, n) in enumerate(nbs):
                    ei = m * 4 + nb
                    gpsimd.wait_ge(vev, ei + 1)
                    gpsimd.dma_start(out=x2o[m * 128:(m + 1) * 128, o:o + n],
                                     in_=x2t[:, (ei % 2) * 512:(ei % 2) * 512 + n]
                                     ).then_inc(sx2[ei % 2], 16)
            if _FFB_DEBUG:
                gpsimd.wait_ge(vev, KD * 4)
                for k in range(MF):
                    gpsimd.dma_start(out=xqo[k * 128:(k + 1) * 128, :],
                                     in_=xqt[:, k * T:(k + 1) * T]).then_inc(sx2[0], 16)
                gpsimd.dma_start(out=wdqo, in_=wdq[:, :]).then_inc(sx2[0], 16)
                gpsimd.wait_ge(sx2[0], (KD * 2 + MF + 1) * 16)
            else:
                gpsimd.wait_ge(sx2a, KD * 2 * 16)
            gpsimd.wait_ge(sx2b, KD * 2 * 16)

    return nc



def build_post1a():
    nc = bass.Bass('TRN2', target_bir_lowering=False, detect_race_conditions=False)
    y1 = nc.dram_tensor('y1', [DI, T], BF16, kind='ExternalInput').ap()
    xi1 = nc.dram_tensor('xi1', [DI, T], BF16, kind='ExternalInput').ap()
    z1 = nc.dram_tensor('z1', [DI, T], BF16, kind='ExternalInput').ap()
    dcol = nc.dram_tensor('dcol', [128, 16], F32, kind='ExternalInput').ap()
    wout = nc.dram_tensor('wout', [DI, D], F32, kind='ExternalInput').ap()
    h1o = nc.dram_tensor('h1o', [D, T], BF16, kind='ExternalOutput').ap()

    NQ = 4
    QW = T // NQ

    def ld_gk(k):
        return (k // 2 + 1) * 48

    def ld_w(k):
        return (k + 1) * 16

    with ExitStack() as ctx:
        yb = ctx.enter_context(nc.sbuf_tensor('yb', [128, 2 * T], BF16))
        xib = ctx.enter_context(nc.sbuf_tensor('xib', [128, 2 * T], BF16))
        zb = ctx.enter_context(nc.sbuf_tensor('zb', [128, 2 * T], BF16))
        szt = ctx.enter_context(nc.sbuf_tensor('szt', [128, T], BF16))
        gtb = ctx.enter_context(nc.sbuf_tensor('gtb', [128, MI * T], BF16))
        wst = ctx.enter_context(nc.sbuf_tensor('wst', [128, D], F32))
        wob = ctx.enter_context(nc.sbuf_tensor('wob', [128, MI * D], BF16))
        o1t = ctx.enter_context(nc.sbuf_tensor('o1t', [128, 2 * QW], BF16))
        dct = ctx.enter_context(nc.sbuf_tensor('dct', [128, 16], F32))
        ps = [ctx.enter_context(nc.psum_tensor(f'ps{i}', [128, QW], F32)) for i in range(8)]
        ld = ctx.enter_context(nc.semaphore('ld'))
        lt0 = ctx.enter_context(nc.semaphore('lt0'))
        lt1 = ctx.enter_context(nc.semaphore('lt1'))
        ldw = ctx.enter_context(nc.semaphore('ldw'))
        sz_ = ctx.enter_context(nc.semaphore('sz'))
        vg = ctx.enter_context(nc.semaphore('vg'))
        wc = ctx.enter_context(nc.semaphore('wc'))
        mmq = ctx.enter_context(nc.semaphore('mmq'))
        sev = ctx.enter_context(nc.semaphore('sev'))
        sx0 = ctx.enter_context(nc.semaphore('sx0'))
        sx1 = ctx.enter_context(nc.semaphore('sx1'))
        block = ctx.enter_context(nc.Block())
        lt = [lt0, lt1]
        sx = [sx0, sx1]

        @block.sync
        def _(sync):
            sync.dma_start(out=dct[:], in_=dcol).then_inc(ld, 16)
            for k in range(MI):
                if k >= 2:
                    sync.wait_ge(vg, k - 1)
                sl = slice((k % 2) * T, (k % 2) * T + T)
                sync.dma_start(out=yb[:, sl], in_=y1[k * 128:(k + 1) * 128, :]).then_inc(lt[k % 2], 16)
                sync.dma_start(out=xib[:, sl], in_=xi1[k * 128:(k + 1) * 128, :]).then_inc(lt[k % 2], 16)
                sync.dma_start(out=zb[:, sl], in_=z1[k * 128:(k + 1) * 128, :]).then_inc(lt[k % 2], 16)
            for k in range(MI):
                if k >= 1:
                    sync.wait_ge(wc, k)
                sync.dma_start(out=wst[:, :], in_=wout[k * 128:(k + 1) * 128, :]).then_inc(ldw, 16)

        @block.scalar
        def _(scalar):
            for k in range(MI):
                scalar.wait_ge(lt[k % 2], ld_gk(k))
                scalar.activation(out=szt[:, :], in_=zb[:, (k % 2) * T:(k % 2) * T + T],
                                  func=ACT.Silu).then_inc(sz_, 1)
                if k < MI - 1:
                    scalar.wait_ge(vg, k + 1)
            for q in range(NQ):
                for m in range(KD):
                    gi = q * KD + m
                    scalar.wait_ge(mmq, (q + 1) * MI)
                    if gi >= 2:
                        scalar.wait_ge(sx[gi % 2], (gi // 2) * 16)
                    scalar.activation(out=o1t[:, (gi % 2) * QW:(gi % 2) * QW + QW],
                                      in_=ps[m][:, :], func=ACT.Gelu).then_inc(sev, 1)

        @block.tensor
        def _(tensor):
            for q in range(NQ):
                for k in range(MI):
                    tensor.wait_ge(vg, k + 1)
                    if k == 0:
                        tensor.wait_ge(wc, MI)
                        if q >= 1:
                            tensor.wait_ge(sev, q * KD)
                    for m in range(KD):
                        ins = tensor.matmul(ps[m][:, :],
                                            wob[:, k * D + m * 128:k * D + (m + 1) * 128],
                                            gtb[:, k * T + q * QW:k * T + q * QW + QW],
                                            start=(k == 0), stop=(k == MI - 1))
                    ins.then_inc(mmq, 1)

        @block.vector
        def _(vector):
            vector.wait_ge(ld, 16)
            for k in range(MI):
                vector.wait_ge(lt[k % 2], ld_gk(k))
                vector.wait_ge(sz_, k + 1)
                sl = slice((k % 2) * T, (k % 2) * T + T)
                gsl = slice(k * T, (k + 1) * T)
                vector.scalar_tensor_tensor(out=gtb[:, gsl], in0=xib[:, sl],
                                            scalar=dct[:, k:k + 1], in1=yb[:, sl],
                                            op0=ALU.mult, op1=ALU.add)
                vector.tensor_tensor(out=gtb[:, gsl], in0=gtb[:, gsl], in1=szt[:, :],
                                     op=ALU.mult).then_inc(vg, 1)
            for k in range(MI):
                vector.wait_ge(ldw, ld_w(k))
                vector.tensor_copy(out=wob[:, k * D:(k + 1) * D], in_=wst[:, :]).then_inc(wc, 1)

        @block.gpsimd
        def _(gpsimd):
            for q in range(NQ):
                for m in range(KD):
                    gi = q * KD + m
                    gpsimd.wait_ge(sev, gi + 1)
                    gpsimd.dma_start(out=h1o[m * 128:(m + 1) * 128, q * QW:(q + 1) * QW],
                                     in_=o1t[:, (gi % 2) * QW:(gi % 2) * QW + QW]
                                     ).then_inc(sx[gi % 2], 16)
            gpsimd.wait_ge(sx0, NQ * KD * 8)
            gpsimd.wait_ge(sx1, NQ * KD * 8)

    return nc


def build_post1b():
    nc = bass.Bass('TRN2', target_bir_lowering=False, detect_race_conditions=False)
    h1f = nc.dram_tensor('h1f', [D, T], BF16, kind='ExternalInput').ap()
    x2f = nc.dram_tensor('x2f', [D, T], F32, kind='ExternalInput').ap()
    gluw = nc.dram_tensor('gluw', [D, DI], F32, kind='ExternalInput').ap()
    glub = nc.dram_tensor('glub', [128, 16], F32, kind='ExternalInput').ap()
    mvd = nc.dram_tensor('mvd', [128, 48], F32, kind='ExternalInput').ap()
    lns = nc.dram_tensor('lns', [2, T], F32, kind='ExternalInput').ap()
    x3o = nc.dram_tensor('x3o', [D, T], F32, kind='ExternalOutput').ap()

    NQ = 4
    QW = T // NQ

    n_small = 4

    def ld_gw(k):
        return (k + 1) * 16

    def ld_x2b(q, m):
        return ((q * KD + m) // 2 + 1) * 16

    with ExitStack() as ctx:
        o1res = ctx.enter_context(nc.sbuf_tensor('o1res', [128, KD * T], BF16))
        gwst = ctx.enter_context(nc.sbuf_tensor('gwst', [128, DI], F32))
        gwb = ctx.enter_context(nc.sbuf_tensor('gwb', [128, KD * DI], BF16))
        x2tb = ctx.enter_context(nc.sbuf_tensor('x2tb', [128, 2 * QW], F32))
        x3t = ctx.enter_context(nc.sbuf_tensor('x3t', [128, 2 * QW], F32))
        sigt = ctx.enter_context(nc.sbuf_tensor('sigt', [128, QW], F32))
        o2t = ctx.enter_context(nc.sbuf_tensor('o2t', [128, QW], F32))
        skt = ctx.enter_context(nc.sbuf_tensor('skt', [128, QW], F32))
        lnr = ctx.enter_context(nc.sbuf_tensor('lnr', [1, 2 * T], F32))
        onesr = ctx.enter_context(nc.sbuf_tensor('onesr', [1, 128], F32))
        gbt = ctx.enter_context(nc.sbuf_tensor('gbt', [128, 16], F32))
        mvt = ctx.enter_context(nc.sbuf_tensor('mvt', [128, 48], F32))
        ps = [ctx.enter_context(nc.psum_tensor(f'ps{i}', [128, QW], F32)) for i in range(4)]
        pM = ctx.enter_context(nc.psum_tensor('pM', [128, QW], F32))
        pA = ctx.enter_context(nc.psum_tensor('pA', [128, QW], F32))
        ld = ctx.enter_context(nc.semaphore('ld'))
        wc = ctx.enter_context(nc.semaphore('wc'))
        mmg = ctx.enter_context(nc.semaphore('mmg'))
        ssig = ctx.enter_context(nc.semaphore('ssig'))
        vev = ctx.enter_context(nc.semaphore('vev'))
        sx0 = ctx.enter_context(nc.semaphore('sx0'))
        sx1 = ctx.enter_context(nc.semaphore('sx1'))
        mbc = ctx.enter_context(nc.semaphore('mbc'))
        von = ctx.enter_context(nc.semaphore('von'))
        ldgw = ctx.enter_context(nc.semaphore('ldgw'))
        lx0 = ctx.enter_context(nc.semaphore('lx0'))
        lx1 = ctx.enter_context(nc.semaphore('lx1'))
        block = ctx.enter_context(nc.Block())
        lx = [lx0, lx1]
        sx = [sx0, sx1]

        @block.sync
        def _(sync):
            sync.dma_start(out=mvt[:], in_=mvd).then_inc(ld, 16)
            sync.dma_start(out=gbt[:], in_=glub).then_inc(ld, 16)
            sync.dma_start(out=lnr[0:1, 0:T], in_=lns[0:1, :]).then_inc(ld, 16)
            sync.dma_start(out=lnr[0:1, T:2 * T], in_=lns[1:2, :]).then_inc(ld, 16)
            for k in range(KD):
                sync.dma_start(out=o1res[:, k * T:(k + 1) * T],
                               in_=h1f[k * 128:(k + 1) * 128, :]).then_inc(ld, 16)
            for k in range(KD):
                if k >= 1:
                    sync.wait_ge(wc, k)
                sync.dma_start(out=gwst[:, :],
                               in_=gluw[k * 128:(k + 1) * 128, :]).then_inc(ldgw, 16)
            for q in range(NQ):
                for m in range(KD):
                    xi_ = q * KD + m
                    if xi_ >= 2:
                        sync.wait_ge(vev, xi_ - 1)
                    sync.dma_start(out=x2tb[:, (xi_ % 2) * QW:(xi_ % 2) * QW + QW],
                                   in_=x2f[m * 128:(m + 1) * 128, q * QW:(q + 1) * QW]
                                   ).then_inc(lx[xi_ % 2], 16)

        @block.scalar
        def _(scalar):
            for q in range(NQ):
                for m in range(KD):
                    gi = q * KD + m
                    scalar.wait_ge(mmg, gi + 1)
                    scalar.activation(out=sigt[:, :], in_=ps[(gi % 2) * 2 + 1][:, :],
                                      func=ACT.Sigmoid,
                                      bias=gbt[:, 8 + m:9 + m]).then_inc(ssig, 1)
                    if gi < NQ * KD - 1:
                        scalar.wait_ge(vev, gi + 1)

        @block.tensor
        def _(tensor):
            for q in range(NQ):
                if q == 0:
                    tensor.wait_ge(ld, (n_small + KD) * 16)
                    tensor.wait_ge(von, 1)
                else:
                    tensor.wait_ge(vev, q * KD)
                tensor.matmul(pM[:, :], onesr[0:1, :], lnr[0:1, q * QW:(q + 1) * QW],
                              start=True, stop=True).then_inc(mbc, 1)
                tensor.matmul(pA[:, :], onesr[0:1, :], lnr[0:1, T + q * QW:T + (q + 1) * QW],
                              start=True, stop=True).then_inc(mbc, 1)
                for m in range(KD):
                    gi = q * KD + m
                    if gi == 0:
                        tensor.wait_ge(ld, (n_small + KD) * 16)
                        tensor.wait_ge(wc, KD)
                    if gi >= 2:
                        tensor.wait_ge(vev, gi - 1)
                        tensor.wait_ge(ssig, gi - 1)
                    pGa = ps[(gi % 2) * 2]
                    pGb = ps[(gi % 2) * 2 + 1]
                    for k in range(KD):
                        i1 = tensor.matmul(pGa[:, :],
                                           gwb[:, k * DI + m * 128:k * DI + (m + 1) * 128],
                                           o1res[:, k * T + q * QW:k * T + q * QW + QW],
                                           start=(k == 0), stop=(k == KD - 1))
                    for k in range(KD):
                        i2 = tensor.matmul(pGb[:, :],
                                           gwb[:, k * DI + (m + 8) * 128:k * DI + (m + 9) * 128],
                                           o1res[:, k * T + q * QW:k * T + q * QW + QW],
                                           start=(k == 0), stop=(k == KD - 1))
                    i2.then_inc(mmg, 1)

        @block.vector
        def _(vector):
            vector.memset(onesr[:], 1.0)
            vector.engine_nop().then_inc(von, 1)
            for k in range(KD):
                vector.wait_ge(ldgw, ld_gw(k))
                vector.tensor_copy(out=gwb[:, k * DI:(k + 1) * DI], in_=gwst[:, :]).then_inc(wc, 1)
            for q in range(NQ):
                for m in range(KD):
                    gi = q * KD + m
                    vector.wait_ge(mmg, gi + 1)
                    vector.wait_ge(ssig, gi + 1)
                    vector.wait_ge(lx[gi % 2], ld_x2b(q, m))
                    if m == 0:
                        vector.wait_ge(mbc, (q + 1) * 2)
                    if gi >= 2:
                        vector.wait_ge(sx[gi % 2], (gi // 2) * 16)
                    pGa = ps[(gi % 2) * 2]
                    sl = slice((gi % 2) * QW, (gi % 2) * QW + QW)
                    vector.scalar_tensor_tensor(out=o2t[:, :], in0=pGa[:, :],
                                                scalar=gbt[:, 0 + m:1 + m], in1=sigt[:, :],
                                                op0=ALU.add, op1=ALU.mult)
                    vector.tensor_tensor(out=skt[:, :], in0=x2tb[:, sl], in1=pM[:, :],
                                         op=ALU.subtract)
                    vector.tensor_tensor(out=skt[:, :], in0=skt[:, :], in1=pA[:, :], op=ALU.mult)
                    vector.tensor_scalar(out=skt[:, :], in0=skt[:, :],
                                         scalar1=mvt[:, 8 + m:9 + m], scalar2=mvt[:, 0 + m:1 + m],
                                         op0=ALU.mult, op1=ALU.add)
                    vector.scalar_tensor_tensor(out=x3t[:, sl], in0=o2t[:, :],
                                                scalar=mvt[:, 16 + m:17 + m], in1=x2tb[:, sl],
                                                op0=ALU.mult, op1=ALU.add)
                    vector.scalar_tensor_tensor(out=x3t[:, sl], in0=skt[:, :],
                                                scalar=mvt[:, 16 + m:17 + m], in1=x3t[:, sl],
                                                op0=ALU.mult, op1=ALU.add).then_inc(vev, 1)

        @block.gpsimd
        def _(gpsimd):
            for q in range(NQ):
                for m in range(KD):
                    gi = q * KD + m
                    gpsimd.wait_ge(vev, gi + 1)
                    gpsimd.dma_start(out=x3o[m * 128:(m + 1) * 128, q * QW:(q + 1) * QW],
                                     in_=x3t[:, (gi % 2) * QW:(gi % 2) * QW + QW]
                                     ).then_inc(sx[gi % 2], 16)
            gpsimd.wait_ge(sx0, NQ * KD * 8)
            gpsimd.wait_ge(sx1, NQ * KD * 8)

    return nc



def build_ffn1():
    nc = bass.Bass('TRN2', target_bir_lowering=False, detect_race_conditions=False)
    h3f = nc.dram_tensor('h3f', [D, T], F32, kind='ExternalInput').ap()
    h3p = nc.dram_tensor('h3p', [128, 8], F32, kind='ExternalInput').ap()
    x3f = nc.dram_tensor('x3f', [D, T], F32, kind='ExternalInput').ap()
    mvd = nc.dram_tensor('mvd', [128, 48], F32, kind='ExternalInput').ap()
    wrT = nc.dram_tensor('wrT', [D, D], F32, kind='ExternalInput').ap()
    wkT = nc.dram_tensor('wkT', [D, D], F32, kind='ExternalInput').ap()
    wvT = nc.dram_tensor('wvT', [D, D], F32, kind='ExternalInput').ap()
    murc = nc.dram_tensor('murc', [128, 8], F32, kind='ExternalInput').ap()
    mukc = nc.dram_tensor('mukc', [128, 8], F32, kind='ExternalInput').ap()
    x4o = nc.dram_tensor('x4o', [D, T], F32, kind='ExternalOutput').ap()

    T1 = T + 1
    nbs = nbl(T)

    def ld_h3(k):
        return (k + 1) * 16

    def ld_w(wi, k):
        return (wi * KD + k + 1) * 16

    def ld_x3(m, nb):
        return ((m * 4 + nb) // 2 + 1) * 16

    with ExitStack() as ctx:
        h3b = ctx.enter_context(nc.sbuf_tensor('h3b', [128, KD * T1], BF16))
        stg = ctx.enter_context(nc.sbuf_tensor('stg', [128, T], F32))
        dsc = ctx.enter_context(nc.sbuf_tensor('dsc', [128, T], BF16))
        xrt = ctx.enter_context(nc.sbuf_tensor('xrt', [128, KD * T], BF16))
        xkt = ctx.enter_context(nc.sbuf_tensor('xkt', [128, KD * T], BF16))
        rres = ctx.enter_context(nc.sbuf_tensor('rres', [128, KD * T], BF16))
        wst = ctx.enter_context(nc.sbuf_tensor('wst', [128, D], F32))
        wrb = ctx.enter_context(nc.sbuf_tensor('wrb', [128, KD * D], BF16))
        wkb = ctx.enter_context(nc.sbuf_tensor('wkb', [128, KD * D], BF16))
        wvb = ctx.enter_context(nc.sbuf_tensor('wvb', [128, KD * D], BF16))
        x3t = ctx.enter_context(nc.sbuf_tensor('x3t', [128, 2 * 512], F32))
        x4t = ctx.enter_context(nc.sbuf_tensor('x4t', [128, 2 * 512], F32))
        vt = ctx.enter_context(nc.sbuf_tensor('vt', [128, 512], F32))
        mvt = ctx.enter_context(nc.sbuf_tensor('mvt', [128, 48], F32))
        mrc = ctx.enter_context(nc.sbuf_tensor('mrc', [128, 8], F32))
        mkc = ctx.enter_context(nc.sbuf_tensor('mkc', [128, 8], F32))
        h3pc = ctx.enter_context(nc.sbuf_tensor('h3pc', [128, 8], F32))
        ps0 = ctx.enter_context(nc.psum_tensor('ps0', [128, 512], F32))
        ps1 = ctx.enter_context(nc.psum_tensor('ps1', [128, 512], F32))
        ps2 = ctx.enter_context(nc.psum_tensor('ps2', [128, 512], F32))
        ps3 = ctx.enter_context(nc.psum_tensor('ps3', [128, 512], F32))
        ld = ctx.enter_context(nc.semaphore('ld'))
        ldh = ctx.enter_context(nc.semaphore('ldh'))
        ldw = ctx.enter_context(nc.semaphore('ldw'))
        lx0 = ctx.enter_context(nc.semaphore('lx0'))
        lx1 = ctx.enter_context(nc.semaphore('lx1'))
        vp = ctx.enter_context(nc.semaphore('vp'))
        wc = ctx.enter_context(nc.semaphore('wc'))
        mmr = ctx.enter_context(nc.semaphore('mmr'))
        sev = ctx.enter_context(nc.semaphore('sev'))
        vev = ctx.enter_context(nc.semaphore('vev'))
        sxa = ctx.enter_context(nc.semaphore('sxa'))
        sxb = ctx.enter_context(nc.semaphore('sxb'))
        vfl = ctx.enter_context(nc.semaphore('vfl'))
        block = ctx.enter_context(nc.Block())

        psum = [ps0, ps1, ps2, ps3]
        lx = [lx0, lx1]
        sx = [sxa, sxb]

        @block.sync
        def _(sync):
            sync.dma_start(out=mvt[:], in_=mvd).then_inc(ld, 16)
            sync.dma_start(out=mrc[:], in_=murc).then_inc(ld, 16)
            sync.dma_start(out=mkc[:], in_=mukc).then_inc(ld, 16)
            sync.dma_start(out=h3pc[:], in_=h3p).then_inc(ld, 16)
            for k in range(KD):
                if k >= 1:
                    sync.wait_ge(vp, 3 * k)
                sync.dma_start(out=stg[:, :], in_=h3f[k * 128:(k + 1) * 128, :]).then_inc(ldh, 16)
            for wi, w in enumerate((wrT, wkT, wvT)):
                for k in range(KD):
                    li = wi * KD + k
                    if li >= 1:
                        sync.wait_ge(wc, li)
                    sync.dma_start(out=wst[:, :], in_=w[k * 128:(k + 1) * 128, :]).then_inc(ldw, 16)
            for m in range(KD):
                for nb, (o, n) in enumerate(nbs):
                    ei = m * 4 + nb
                    if ei >= 2:
                        sync.wait_ge(vev, ei - 1)
                    sync.dma_start(out=x3t[:, (ei % 2) * 512:(ei % 2) * 512 + n],
                                   in_=x3f[m * 128:(m + 1) * 128, o:o + n]).then_inc(lx[ei % 2], 16)

        @block.scalar
        def _(scalar):
            for m in range(KD):
                for nb, (o, n) in enumerate(nbs):
                    ei = m * 4 + nb
                    scalar.wait_ge(mmr, ei + 1)
                    scalar.activation(out=rres[:, m * T + o:m * T + o + n],
                                      in_=psum[ei % 4][:, :n], func=ACT.Sigmoid).then_inc(sev, 1)
            for m in range(KD):
                for nb, (o, n) in enumerate(nbs):
                    ei = 32 + m * 4 + nb
                    scalar.wait_ge(mmr, ei + 1)
                    scalar.activation(out=xrt[:, m * T + o:m * T + o + n],
                                      in_=psum[ei % 4][:, :n], func=ACT.Relu)
                    scalar.activation(out=xrt[:, m * T + o:m * T + o + n],
                                      in_=xrt[:, m * T + o:m * T + o + n],
                                      func=ACT.Square).then_inc(sev, 1)

        @block.tensor
        def _(tensor):
            for m in range(KD):
                for nb, (o, n) in enumerate(nbs):
                    ei = m * 4 + nb
                    if ei == 0:
                        tensor.wait_ge(wc, KD)
                        tensor.wait_ge(vp, 3 * KD)
                    if ei >= 4:
                        tensor.wait_ge(sev, ei - 3)
                    for k in range(KD):
                        ins = tensor.matmul(psum[ei % 4][:, :n],
                                            wrb[:, k * D + m * 128:k * D + (m + 1) * 128],
                                            xrt[:, k * T + o:k * T + o + n],
                                            start=(k == 0), stop=(k == KD - 1))
                    ins.then_inc(mmr, 1)
            for m in range(KD):
                for nb, (o, n) in enumerate(nbs):
                    ei = 32 + m * 4 + nb
                    if ei == 32:
                        tensor.wait_ge(wc, 2 * KD)
                        tensor.wait_ge(sev, 32)
                    if ei >= 4:
                        tensor.wait_ge(sev, ei - 3)
                    for k in range(KD):
                        ins = tensor.matmul(psum[ei % 4][:, :n],
                                            wkb[:, k * D + m * 128:k * D + (m + 1) * 128],
                                            xkt[:, k * T + o:k * T + o + n],
                                            start=(k == 0), stop=(k == KD - 1))
                    ins.then_inc(mmr, 1)
            for m in range(KD):
                for nb, (o, n) in enumerate(nbs):
                    ei = 64 + m * 4 + nb
                    if ei == 64:
                        tensor.wait_ge(wc, 3 * KD)
                        tensor.wait_ge(sev, 64)
                    if ei - 64 >= 4:
                        tensor.wait_ge(vev, ei - 64 - 3)
                    for k in range(KD):
                        ins = tensor.matmul(psum[ei % 4][:, :n],
                                            wvb[:, k * D + m * 128:k * D + (m + 1) * 128],
                                            xrt[:, k * T + o:k * T + o + n],
                                            start=(k == 0), stop=(k == KD - 1))
                    ins.then_inc(mmr, 1)

        @block.vector
        def _(vector):
            vector.wait_ge(ld, 64)
            vfl_c = [0]

            def vsync(ins):
                vfl_c[0] += 1
                ins.then_inc(vfl, 1)
                vector.wait_ge(vfl, vfl_c[0])

            for k in range(KD):
                vector.wait_ge(ldh, ld_h3(k))
                sl0 = k * T1
                vector.tensor_copy(out=h3b[:, sl0 + 1:sl0 + T1], in_=stg[:, :])
                vsync(vector.tensor_copy(out=h3b[:, sl0:sl0 + 1], in_=h3pc[:, k:k + 1]))
                vector.engine_nop().then_inc(vp, 1)
                vector.tensor_tensor(out=dsc[:, :], in0=h3b[:, sl0 + 1:sl0 + T1],
                                     in1=h3b[:, sl0:sl0 + T], op=ALU.subtract).then_inc(vp, 1)
                vector.scalar_tensor_tensor(out=xrt[:, k * T:(k + 1) * T], in0=dsc[:, :],
                                            scalar=mrc[:, k:k + 1], in1=h3b[:, sl0:sl0 + T],
                                            op0=ALU.mult, op1=ALU.add)
                vector.scalar_tensor_tensor(out=xkt[:, k * T:(k + 1) * T], in0=dsc[:, :],
                                            scalar=mkc[:, k:k + 1], in1=h3b[:, sl0:sl0 + T],
                                            op0=ALU.mult, op1=ALU.add).then_inc(vp, 1)
            for wi, dst in enumerate((wrb, wkb, wvb)):
                for k in range(KD):
                    vector.wait_ge(ldw, ld_w(wi, k))
                    vector.tensor_copy(out=dst[:, k * D:(k + 1) * D], in_=wst[:, :]).then_inc(wc, 1)
            for m in range(KD):
                for nb, (o, n) in enumerate(nbs):
                    ei = m * 4 + nb
                    vector.wait_ge(mmr, 64 + ei + 1)
                    vector.wait_ge(lx[ei % 2], ld_x3(m, nb))
                    if ei >= 2:
                        vector.wait_ge(sx[ei % 2], (ei // 2) * 16)
                    vector.tensor_tensor(out=vt[:, :n], in0=psum[ei % 4][:, :n],
                                         in1=rres[:, m * T + o:m * T + o + n], op=ALU.mult)
                    vector.scalar_tensor_tensor(out=x4t[:, (ei % 2) * 512:(ei % 2) * 512 + n],
                                                in0=vt[:, :n], scalar=mvt[:, 40 + m:41 + m],
                                                in1=x3t[:, (ei % 2) * 512:(ei % 2) * 512 + n],
                                                op0=ALU.mult, op1=ALU.add).then_inc(vev, 1)

        @block.gpsimd
        def _(gpsimd):
            for m in range(KD):
                for nb, (o, n) in enumerate(nbs):
                    ei = m * 4 + nb
                    gpsimd.wait_ge(vev, ei + 1)
                    gpsimd.dma_start(out=x4o[m * 128:(m + 1) * 128, o:o + n],
                                     in_=x4t[:, (ei % 2) * 512:(ei % 2) * 512 + n]
                                     ).then_inc(sx[ei % 2], 16)
            gpsimd.wait_ge(sxa, KD * 2 * 16)
            gpsimd.wait_ge(sxb, KD * 2 * 16)

    return nc



def _fmcols(v):
    v = np.ascontiguousarray(v).reshape(-1)
    return np.ascontiguousarray(v.reshape(-1, 128).T)


def _prep_common(inp):
    P = {}
    x0 = np.ascontiguousarray(np.asarray(inp['L0_attn_hidden_states'])[0].T)
    P['x0fm'] = x0
    P['modw0'] = np.ascontiguousarray(np.asarray(inp['L0_wmod']))
    P['modw1'] = np.ascontiguousarray(np.asarray(inp['L1_wmod']))
    P['c0fm'] = _fmcols(np.asarray(inp['L0_c'])[0])
    P['c1fm'] = _fmcols(np.asarray(inp['L1_c'])[0])
    P['w0inT'] = np.ascontiguousarray(np.asarray(inp['L0_attn_in_proj_w']).T)
    P['cw0'] = np.ascontiguousarray(
        np.asarray(inp['L0_attn_conv_w']).reshape(MI, 128, 4).transpose(1, 0, 2).reshape(128, 64))
    P['cb0'] = _fmcols(np.asarray(inp['L0_attn_conv_b']))
    P['xw0T'] = np.ascontiguousarray(np.asarray(inp['L0_attn_x_proj_w']).T)
    P['dtw0T'] = np.ascontiguousarray(np.asarray(inp['L0_attn_dt_proj_w']).T)
    P['alog0'] = np.asarray(inp['L0_attn_A_log'])
    P['selm'] = np.ascontiguousarray(np.kron(np.eye(32, dtype=np.float32),
                                             np.ones((1, 128), np.float32)))
    P['d0'] = _fmcols(np.asarray(inp['L0_attn_D']))
    P['w0outT'] = np.ascontiguousarray(np.asarray(inp['L0_attn_out_proj_w']).T)
    P['wg'] = np.asarray(inp['L0_ffn_w_gate'])
    P['wu'] = np.asarray(inp['L0_ffn_w_up'])
    P['wd'] = np.asarray(inp['L0_ffn_w_down'])
    P['ident'] = np.ascontiguousarray(np.eye(128, dtype=np.float32).astype(NBF))
    P['clipx'] = np.asarray(inp['L0_ffn_clip_x']).reshape(1, 1)
    P['clipd'] = np.asarray(inp['L0_ffn_clip_down']).reshape(1, 1)
    P['gam1'] = _fmcols(np.asarray(inp['L1_attn_ln_gamma']))
    P['bet1'] = _fmcols(np.asarray(inp['L1_attn_ln_beta']))
    P['w1in'] = np.ascontiguousarray(np.asarray(inp['L1_attn_in_proj_w']))
    P['cw1'] = np.ascontiguousarray(
        np.asarray(inp['L1_attn_conv_w']).reshape(MI, 128, 4).transpose(1, 0, 2).reshape(128, 64))
    P['cb1'] = _fmcols(np.asarray(inp['L1_attn_conv_b']))
    P['xw1'] = np.ascontiguousarray(np.asarray(inp['L1_attn_x_proj_w']))
    P['dtw1'] = np.ascontiguousarray(np.asarray(inp['L1_attn_dt_proj_w']))
    P['dtb1'] = np.asarray(inp['L1_attn_dt_proj_b'])
    P['alog1'] = np.asarray(inp['L1_attn_A_log'])
    P['d1'] = _fmcols(np.asarray(inp['L1_attn_D']))
    P['w1out'] = np.ascontiguousarray(np.asarray(inp['L1_attn_out_proj_w']))
    P['gluw'] = np.ascontiguousarray(np.asarray(inp['L1_attn_glu_w']))
    P['glub'] = _fmcols(np.asarray(inp['L1_attn_glu_b']))
    P['wrT'] = np.ascontiguousarray(np.asarray(inp['L1_ffn_W_r']).T)
    P['wkT'] = np.ascontiguousarray(np.asarray(inp['L1_ffn_W_k']).T)
    P['wvT'] = np.ascontiguousarray(np.asarray(inp['L1_ffn_W_v']).T)
    P['mur'] = _fmcols(np.asarray(inp['L1_ffn_mu_r']))
    P['muk'] = _fmcols(np.asarray(inp['L1_ffn_mu_k']))
    return P


def _halo_slice(xfm, j):
    lo = j * T - HALO
    if lo >= 0:
        return np.ascontiguousarray(xfm[:, lo:j * T + T])
    pad = np.zeros((xfm.shape[0], -lo), np.float32)
    return np.ascontiguousarray(np.concatenate([pad, xfm[:, 0:j * T + T]], axis=1))


def _hs(j):
    v = np.zeros((128, 1), np.float32) if j == 0 else np.ones((128, 1), np.float32)
    return v


def run_modvec(P):
    ncb = build_modvec()
    JS = 6144 // NC
    maps = [dict(w0=np.ascontiguousarray(P['modw0'][:, j * JS:(j + 1) * JS]),
                 w1=np.ascontiguousarray(P['modw1'][:, j * JS:(j + 1) * JS]),
                 c0=P['c0fm'], c1=P['c1fm']) for j in range(NC)]
    res = run_bass_kernel_spmd(ncb, maps, core_ids=list(range(NC)))
    JC = JS // 128
    mv0 = np.concatenate([res.results[j]['mvo'][:, 0:JC] for j in range(NC)], axis=1)
    mv1 = np.concatenate([res.results[j]['mvo'][:, JC:2 * JC] for j in range(NC)], axis=1)
    return np.ascontiguousarray(mv0), np.ascontiguousarray(mv1)


def run_pre(P, layer, xfm, mvin):
    ncb = build_pre(layer)
    maps = []
    for j in range(NC):
        m = dict(xin=_halo_slice(xfm, j), hs=_hs(j), mvin=mvin)
        if layer == 0:
            m.update(winT=P['w0inT'], cw=P['cw0'], cb=P['cb0'], xwT=P['xw0T'])
        else:
            m.update(winT=P['w1in'], cw=P['cw1'], cb=P['cb1'], xwT=P['xw1'],
                     gam=P['gam1'], bet=P['bet1'])
        maps.append(m)
    res = run_bass_kernel_spmd(ncb, maps, core_ids=list(range(NC)))
    return res.results


def run_pre_full(P, layer, x_parts, mvin):
    ncb = build_pre(layer)
    maps = []
    for j in range(NC):
        if j == 0:
            pad = np.zeros((D, HALO), np.float32)
            xin = np.concatenate([pad, x_parts[0]], axis=1)
        else:
            xin = np.concatenate([x_parts[j - 1][:, T - HALO:], x_parts[j]], axis=1)
        m = dict(xin=np.ascontiguousarray(xin), hs=_hs(j), mvin=mvin)
        if layer == 0:
            m.update(winT=P['w0inT'], cw=P['cw0'], cb=P['cb0'], xwT=P['xw0T'])
        else:
            m.update(winT=P['w1in'], cw=P['cw1'], cb=P['cb1'], xwT=P['xw1'],
                     gam=P['gam1'], bet=P['bet1'])
        maps.append(m)
    res = run_bass_kernel_spmd(ncb, maps, core_ids=list(range(NC)))
    return res.results


def kernel(**inputs):
    P = _prep_common(inputs)
    x0_parts = [np.ascontiguousarray(P['x0fm'][:, j * T:(j + 1) * T]) for j in range(NC)]

    mv0, mv1 = run_modvec(P)

    r = run_pre_full(P, 0, x0_parts, mv0)
    mvd0 = np.ascontiguousarray(r[0]['mvdo'])
    xi0_parts = [np.ascontiguousarray(r[j]['xio']) for j in range(NC)]
    z0_parts = [np.ascontiguousarray(r[j]['zo']) for j in range(NC)]
    dtbc0_parts = [np.ascontiguousarray(r[j]['dtbco']) for j in range(NC)]

    y0 = run_scan(P, 0, xi0_parts, dtbc0_parts)

    ncb = build_post0()
    maps = [dict(x0f=x0_parts[j],
                 y0=np.ascontiguousarray(y0[:, j * T:(j + 1) * T]),
                 xi0=xi0_parts[j], z0=z0_parts[j],
                 dcol=P['d0'], woutT=P['w0outT'], mvd=mvd0) for j in range(NC)]
    r = run_bass_kernel_spmd(ncb, maps, core_ids=list(range(NC)))
    x1_parts = [np.ascontiguousarray(r.results[j]['x1o']) for j in range(NC)]

    ncb = build_modln()
    maps = [dict(xf=x1_parts[j], mvd=mvd0) for j in range(NC)]
    r = run_bass_kernel_spmd(ncb, maps, core_ids=list(range(NC)))
    h1_parts = [np.ascontiguousarray(r.results[j]['ho']) for j in range(NC)]
    gmax1 = np.ascontiguousarray(np.concatenate(
        [r.results[j]['lmax'].T.reshape(1, D) for j in range(NC)], axis=1))

    ncb = build_ffn0a()
    clipx = np.zeros((1, 32), np.float32)
    clipx[0, 0] = P['clipx'][0, 0]
    maps = [dict(h1f=h1_parts[j], gmaxa=gmax1, clipx=clipx,
                 wg=P['wg'], wu=P['wu'], ident=P['ident']) for j in range(NC)]
    r = run_bass_kernel_spmd(ncb, maps, core_ids=list(range(NC)))
    h2_parts = [np.ascontiguousarray(r.results[j]['h2o']) for j in range(NC)]
    gmax2 = np.ascontiguousarray(np.concatenate(
        [r.results[j]['rm2o'].T.reshape(1, FF) for j in range(NC)], axis=1))

    ncb = build_ffn0b()
    clipd = np.zeros((1, 32), np.float32)
    clipd[0, 0] = P['clipd'][0, 0]
    maps = [dict(h2f=h2_parts[j], gmaxa=gmax2, clipd=clipd,
                 wd=P['wd'], x1f=x1_parts[j], mvd=mvd0,
                 ident=P['ident']) for j in range(NC)]
    r = run_bass_kernel_spmd(ncb, maps, core_ids=list(range(NC)))
    x2_parts = [np.ascontiguousarray(r.results[j]['x2o']) for j in range(NC)]

    r = run_pre_full(P, 1, x2_parts, mv1)
    mvd1 = np.ascontiguousarray(r[0]['mvdo'])
    xi1_parts = [np.ascontiguousarray(r[j]['xio']) for j in range(NC)]
    z1_parts = [np.ascontiguousarray(r[j]['zo']) for j in range(NC)]
    dtbc1_parts = [np.ascontiguousarray(r[j]['dtbco']) for j in range(NC)]
    lns1_parts = [np.ascontiguousarray(r[j]['lnso']) for j in range(NC)]

    y1 = run_scan(P, 1, xi1_parts, dtbc1_parts)

    glub_fm = np.ascontiguousarray(
        np.asarray(inputs['L1_attn_glu_b']).reshape(2, 8, 128).transpose(2, 0, 1).reshape(128, 16))
    ncb = build_post1a()
    maps = [dict(y1=np.ascontiguousarray(y1[:, j * T:(j + 1) * T]),
                 xi1=xi1_parts[j], z1=z1_parts[j],
                 dcol=P['d1'], wout=P['w1out']) for j in range(NC)]
    r = run_bass_kernel_spmd(ncb, maps, core_ids=list(range(NC)))
    h1g_parts = [np.ascontiguousarray(r.results[j]['h1o']) for j in range(NC)]

    ncb = build_post1b()
    maps = [dict(h1f=h1g_parts[j], x2f=x2_parts[j],
                 gluw=P['gluw'], glub=glub_fm,
                 mvd=mvd1, lns=lns1_parts[j]) for j in range(NC)]
    r = run_bass_kernel_spmd(ncb, maps, core_ids=list(range(NC)))
    x3_parts = [np.ascontiguousarray(r.results[j]['x3o']) for j in range(NC)]

    ncb = build_modln()
    maps = [dict(xf=x3_parts[j], mvd=mvd1) for j in range(NC)]
    r = run_bass_kernel_spmd(ncb, maps, core_ids=list(range(NC)))
    h3_parts = [np.ascontiguousarray(r.results[j]['ho']) for j in range(NC)]

    ncb = build_ffn1()
    maps = []
    for j in range(NC):
        h3prev = h3_parts[j - 1][:, T - 1] if j > 0 else np.zeros(D, np.float32)
        maps.append(dict(h3f=h3_parts[j], h3p=_fmcols(h3prev.astype(np.float32)),
                         x3f=x3_parts[j], mvd=mvd1,
                         wrT=P['wrT'], wkT=P['wkT'], wvT=P['wvT'],
                         murc=P['mur'], mukc=P['muk']))
    r = run_bass_kernel_spmd(ncb, maps, core_ids=list(range(NC)))
    x4 = np.concatenate([r.results[j]['x4o'] for j in range(NC)], axis=1)
    return np.ascontiguousarray(x4.T)[None].astype(np.float32)
